# version 09
# speedup vs baseline: 2.1494x; 1.0028x over previous
import math
from contextlib import ExitStack

import numpy as np
import ml_dtypes
import os as _os
import concourse.bass as bass
import concourse.mybir as mybir
from concourse.bass_utils import run_bass_kernel_spmd

F32 = mybir.dt.float32
BF16 = mybir.dt.bfloat16
I32 = mybir.dt.int32
ALU = mybir.AluOpType
AF = mybir.ActivationFunctionType
AX = mybir.AxisListType.X
AP = bass.AP

NCORE = 8
NTOK = 4096
NT = 32
D = 1024
HEADS_A, DH_A = 8, 128
HEADS_F, DH_F = 8, 64
WIN = 512
HID = 2730
HIDP = 2816
NJ = HIDP // 128

def _trunc12(x):
    u = np.array(np.float32(x), np.float32).view(np.uint32) & np.uint32(0xFFFFF000)
    return float(np.array(u, np.uint32).view(np.float32))

_PI = float(np.pi)
P1C = _trunc12(_PI)
P2C = _trunc12(_PI - P1C)
P3C = float(np.float32(_PI - P1C - P2C))
H1C = float(np.float32(_PI / 2))
H2C = float(np.float32(_PI / 2 - H1C))
INV_PI = float(np.float32(1.0 / _PI))
L1C = _trunc12(math.log(10000.0))
L2C = _trunc12(math.log(10000.0) - L1C)
L3C = float(np.float32(math.log(10000.0) - L1C - L2C))
C_HALFPI = 1.5707964
NEG_BIG = -1e9
SCALE_ATTN = 128.0 ** -0.5

_TAPS = ()
_PHASES = 99

_BUILD_CACHE = {}


class Sem:
    __slots__ = ("h", "n", "base")

    def __init__(self, nc, es, name):
        self.h = es.enter_context(nc.semaphore(name))
        self.n = 0
        self.base = 0

    def rebase(self):
        self.base = self.n
        return self

    def inc(self, instr, k=1):
        instr.then_inc(self.h, k)
        self.n += k
        return instr

    def wait(self, eng, n=None):
        eng.wait_ge(self.h, self.n if n is None else self.base + n)


class Ring:

    def __init__(self, ctx, name, slots):
        self.sems = [ctx.psem(f"{name}{i}") for i in range(slots)]
        self.slots = slots

    def inc(self, instr, i, k=16):
        self.sems[i % self.slots].inc(instr, k)

    def wait(self, eng, i, k=16):
        s = self.sems[i % self.slots]
        eng.wait_ge(s.h, s.base + k * (i // self.slots + 1))

    def wait_all_issued(self, eng):
        for s in self.sems:
            s.wait(eng)


class Ctx:
    def __init__(self, nc, es, block):
        self.nc = nc
        self.es = es
        self.block = block
        self.sems = {}
        self.dram = {}
        self.sb = {}

    def sem(self, name):
        if name not in self.sems:
            self.sems[name] = Sem(self.nc, self.es, name)
        return self.sems[name]

    def psem(self, name):
        return self.sem(name).rebase()

    def barrier(self, name):
        s = self.sem("bar")
        b = self.block
        cur = [(ss.h, ss.n) for nm, ss in self.sems.items() if nm != "bar"]
        for sec in (b.sync, b.gpsimd, b.vector, b.scalar, b.tensor):
            @sec
            def _(eng, cur=cur):
                for h, n in cur:
                    if n:
                        eng.wait_ge(h, n)

        @b.vector
        def _(v, s=s):
            s.inc(v.memset(self.sb["barv"][:1, 0:1], 0.0))

        @b.scalar
        def _(sc, s=s):
            s.inc(sc.activation(self.sb["barv"][:1, 1:2], self.sb["barv"][:1, 1:2],
                                AF.Copy, scale=0.0))

        @b.gpsimd
        def _(g, s=s):
            s.inc(g.memset(self.sb["barv"][:1, 2:3], 0.0))

        @b.sync
        def _(eng, s=s):
            s.inc(eng.nop())

        @b.tensor
        def _(eng, s=s):
            s.inc(eng.nop())
        tgt = s.n
        for sec in (b.sync, b.gpsimd, b.vector, b.scalar, b.tensor):
            @sec
            def _(eng, s=s, tgt=tgt):
                eng.wait_ge(s.h, tgt)



def build_nc(taps=(), phases=99):
    nc = bass.Bass("TRN2", target_bir_lowering=False, detect_race_conditions=False)
    es = ExitStack()

    def dram(name, shape, dt, kind=None):
        k = kind
        if k is None:
            k = "ExternalOutput" if name in taps else "Internal"
        if k == "Internal":
            t = nc.dram_tensor(name, shape, dt)
        else:
            t = nc.dram_tensor(name, shape, dt, kind=k)
        return t.ap()

    io = {}
    io["x"] = dram("x", [NTOK, D], F32, "ExternalInput")
    io["wqkvA"] = dram("wqkvA", [D, 3 * D], F32, "ExternalInput")
    io["gammaA"] = dram("gammaA", [D], F32, "ExternalInput")
    io["woutT"] = dram("woutT", [D, D], F32, "ExternalInput")
    for l in range(2):
        io[f"wqkvF{l}"] = dram(f"wqkvF{l}", [D, 1536], F32, "ExternalInput")
        io[f"qs{l}"] = dram(f"qs{l}", [64], F32, "ExternalInput")
        io[f"ks{l}"] = dram(f"ks{l}", [64], F32, "ExternalInput")
        io[f"wo{l}"] = dram(f"wo{l}", [512, D], F32, "ExternalInput")
        io[f"lng{l}"] = dram(f"lng{l}", [D], F32, "ExternalInput")
        io[f"lnb{l}"] = dram(f"lnb{l}", [D], F32, "ExternalInput")
        io[f"w1a{l}"] = dram(f"w1a{l}", [D, HIDP], F32, "ExternalInput")
        io[f"w1g{l}"] = dram(f"w1g{l}", [D, HIDP], F32, "ExternalInput")
        io[f"b1a{l}"] = dram(f"b1a{l}", [HIDP], F32, "ExternalInput")
        io[f"b1g{l}"] = dram(f"b1g{l}", [HIDP], F32, "ExternalInput")
        io[f"w2{l}"] = dram(f"w2{l}", [HIDP, D], F32, "ExternalInput")
        io[f"b2{l}"] = dram(f"b2{l}", [D], F32, "ExternalInput")
    io["out"] = dram("out", [NTOK, D], F32, "ExternalOutput")

    io["z_bf"] = dram("z_bf", [NTOK, D], BF16)
    io["qkvA_bf"] = dram("qkvA_bf", [NTOK, 3 * D], BF16)
    io["x1_f"] = dram("x1_f", [NTOK, D], F32)
    io["yA_f"] = dram("yA_f", [NTOK, D], F32)
    io["yB_f"] = dram("yB_f", [NTOK, D], F32)
    io["yA_bf"] = dram("yA_bf", [NTOK, D], BF16)
    io["z2_bf"] = dram("z2_bf", [NTOK, D], BF16)
    io["qkvF_bf"] = dram("qkvF_bf", [NTOK, 1536], BF16)
    io["qr_bf"] = dram("qr_bf", [NTOK, 512], BF16)
    io["kr_bf"] = dram("kr_bf", [NTOK, 512], BF16)

    with es:
        block = es.enter_context(nc.Block())
        ctx = Ctx(nc, es, block)
        ctx.dram = io

        sb = ctx.sb
        sb["mdiag"] = es.enter_context(nc.sbuf_tensor("mdiag", [128, 128], F32))
        sb["mfar"] = es.enter_context(nc.sbuf_tensor("mfar", [128, 128], F32))
        sb["ident"] = es.enter_context(nc.sbuf_tensor("ident", [128, 128], BF16))
        sb["ones1"] = es.enter_context(nc.sbuf_tensor("ones1", [1, 128], F32))
        sb["barv"] = es.enter_context(nc.sbuf_tensor("barv", [1, 4], F32))
        CONSTS = [1e-5, 1e-30]
        sb["cpool"] = es.enter_context(nc.sbuf_tensor("cpool", [128, len(CONSTS)], F32))
        for i, cv in enumerate(CONSTS):
            nc.const_aps.register(F32, cv, sb["cpool"][:, i:i + 1])
        ctx.cpool_consts = CONSTS

        def run_phase(idx, name, fn, *args):
            if phases > idx:
                with ExitStack() as ps:
                    ctx.phase_stack = ps
                    fn(ctx, *args)
            ctx.barrier(name)

        run_phase(0, "p0", emit_p0)
        run_phase(1, "p1a", emit_p1a_tables)
        run_phase(2, "p1b", emit_p1b)
        run_phase(3, "p1c", emit_p1c)
        run_phase(4, "p2pre", emit_p2pre)
        for l in range(2):
            run_phase(5 + 2 * l, f"p2a{l}", emit_p2a, l)
            run_phase(6 + 2 * l, f"p2b{l}", emit_p2b, l)
        st = ctx.sem("out_store")

        @block.sync
        def _(sync):
            sync.wait_ge(st.h, st.n)

    return nc



def emit_p1a_tables(ctx):
    if _os.environ.get("K_NO_TABLES") != "1":
        emit_tables(ctx)
    emit_p1a(ctx)


def emit_p0(ctx):
    nc, block = ctx.nc, ctx.block
    ph = ctx.phase_stack
    sb = ctx.sb
    g_done = ctx.sem("p0_g")
    idm = ph.enter_context(nc.sbuf_tensor("p0_idm", [128, 128], F32))

    @block.gpsimd
    def _(g):
        g_done.inc(g.iota(idm[:], [[-1, 128]], channel_multiplier=1,
                          allow_small_or_imprecise_dtypes=True))
        for i, cv in enumerate(ctx.cpool_consts):
            g_done.inc(g.memset(ctx.sb["cpool"][:, i:i + 1], cv))

    @block.vector
    def _(v):
        g_done.wait(v)
        v.tensor_single_scalar(sb["ident"][:], idm[:], 0.0, op=ALU.is_equal)
        v.tensor_single_scalar(sb["mdiag"][:], idm[:], 0.0, op=ALU.is_lt)
        v.tensor_single_scalar(sb["mfar"][:], idm[:], 0.0, op=ALU.is_gt)
        v.drain()
        v.tensor_scalar_mul(sb["mdiag"][:], sb["mdiag"][:], NEG_BIG)
        v.tensor_scalar_mul(sb["mfar"][:], sb["mfar"][:], NEG_BIG)
        v.memset(sb["ones1"][:], 1.0)


def emit_tables(ctx):
    nc, block = ctx.nc, ctx.block
    ph = ctx.phase_stack
    es = ph
    sb = ctx.sb
    sb["Tcos"] = ph.enter_context(nc.sbuf_tensor("Tcos", [128, NT * 32], F32))
    sb["Tsin"] = ph.enter_context(nc.sbuf_tensor("Tsin", [128, NT * 32], F32))
    sb["Txs"] = ph.enter_context(nc.sbuf_tensor("Txs", [128, NT * 32], F32))
    sb["Trxs"] = ph.enter_context(nc.sbuf_tensor("Trxs", [128, NT * 32], F32))
    ctx.dram["tab_f"] = nc.dram_tensor("tab_f", [4 * 128, NT * 32], F32, kind="Internal").ap()
    g_done = ctx.sem("t0_g")
    v_done = ctx.sem("t0_v")
    s_done = ctx.sem("t0_s")

    t = {}
    for name, shape, dt in [
        ("jrow", [128, 32], F32), ("ptile", [128, 1], F32),
        ("u1", [128, 32], F32), ("u2", [128, 32], F32),
        ("ivf", [128, 32], F32), ("lnb", [128, 32], F32),
        ("ang", [128, 32], F32), ("angc", [128, 32], F32),
        ("yy", [128, 32], F32), ("mi", [128, 32], I32), ("mf", [128, 32], F32),
        ("rr", [128, 32], F32), ("a1", [128, 32], F32), ("a2", [128, 32], F32),
        ("hh", [128, 32], F32), ("hi", [128, 32], I32), ("sg", [128, 32], F32),
        ("sn", [128, 32], F32), ("pw", [128, 1], F32),
    ]:
        t[name] = es.enter_context(nc.sbuf_tensor("t0_" + name, shape, dt))

    @block.gpsimd
    def _(g):
        g_done.inc(g.iota(t["jrow"][:], [[1, 32]], channel_multiplier=0,
                          allow_small_or_imprecise_dtypes=True))

    @block.vector
    def _(v):
        g_done.wait(v)
        v.tensor_scalar(t["u1"][:], t["jrow"][:], L1C, None, op0=ALU.mult)
        v.tensor_scalar(t["u2"][:], t["jrow"][:], L2C, None, op0=ALU.mult)
        v.drain()
        v.tensor_add(t["u1"][:], t["u1"][:], t["u2"][:])
        v.tensor_scalar(t["u2"][:], t["jrow"][:], L3C, None, op0=ALU.mult)
        v.drain()
        v_done.inc(v.tensor_add(t["u1"][:], t["u1"][:], t["u2"][:]))

    @block.scalar
    def _(s):
        v_done.wait(s)
        s_done.inc(s.activation(t["ivf"][:], t["u1"][:], AF.Exp, scale=-0.03125))

    @block.vector
    def _(v):
        s_done.wait(v)
        v.tensor_scalar(t["u2"][:], t["jrow"][:], 10.0, 128.0, op0=ALU.mult, op1=ALU.add)
        v.drain()
        v_done.inc(v.tensor_scalar(t["u2"][:], t["u2"][:], 1.0 / 448.0, None, op0=ALU.mult))

    @block.scalar
    def _(s):
        v_done.wait(s)
        s_done.inc(s.activation(t["lnb"][:], t["u2"][:], AF.Ln))

    def emit_sin(steps, x_name, out_ap):
        def v1(v):
            v.tensor_scalar(t["yy"][:], t[x_name][:], INV_PI, 0.5, op0=ALU.mult, op1=ALU.add)
        steps.append(("v", v1))
        def s1(s):
            s.copy(t["mi"][:], t["yy"][:])
            s.drain()
            s.copy(t["mf"][:], t["mi"][:])
        steps.append(("s", s1))
        def v2(v):
            v.scalar_tensor_tensor(t["rr"][:], t["mf"][:], -P1C, t[x_name][:], op0=ALU.mult, op1=ALU.add)
            v.drain()
            v.scalar_tensor_tensor(t["rr"][:], t["mf"][:], -P2C, t["rr"][:], op0=ALU.mult, op1=ALU.add)
            v.drain()
            v.scalar_tensor_tensor(t["rr"][:], t["mf"][:], -P3C, t["rr"][:], op0=ALU.mult, op1=ALU.add)
            v.drain()
            v.tensor_single_scalar(t["a1"][:], t["rr"][:], C_HALFPI, op=ALU.is_ge)
            v.tensor_single_scalar(t["a2"][:], t["rr"][:], -C_HALFPI, op=ALU.is_le)
            v.drain()
            v.tensor_sub(t["a1"][:], t["a1"][:], t["a2"][:])
            v.drain()
            v.tensor_add(t["mf"][:], t["mf"][:], t["a1"][:])
            v.scalar_tensor_tensor(t["rr"][:], t["a1"][:], -P1C, t["rr"][:], op0=ALU.mult, op1=ALU.add)
            v.drain()
            v.scalar_tensor_tensor(t["rr"][:], t["a1"][:], -P2C, t["rr"][:], op0=ALU.mult, op1=ALU.add)
            v.drain()
            v.scalar_tensor_tensor(t["rr"][:], t["a1"][:], -P3C, t["rr"][:], op0=ALU.mult, op1=ALU.add)
            v.drain()
            v.tensor_scalar(t["hh"][:], t["mf"][:], 0.5, None, op0=ALU.mult)
        steps.append(("v", v2))
        def s2(s):
            s.copy(t["hi"][:], t["hh"][:])
            s.drain()
            s.copy(t["sg"][:], t["hi"][:])
        steps.append(("s", s2))
        def v3(v):
            v.tensor_sub(t["sg"][:], t["hh"][:], t["sg"][:])
        steps.append(("v", v3))
        def s3(s):
            s.activation(t["sg"][:], t["sg"][:], AF.Abs)
            s.drain()
            s.activation(t["sn"][:], t["rr"][:], AF.Sin)
        steps.append(("s", s3))
        def v4(v, out_ap=out_ap):
            v.tensor_scalar(t["sg"][:], t["sg"][:], -4.0, 1.0, op0=ALU.mult, op1=ALU.add)
            v.drain()
            v.tensor_mul(out_ap, t["sn"][:], t["sg"][:])
        steps.append(("v", v4))

    for ti in range(NT):
        cols = slice(ti * 32, (ti + 1) * 32)
        steps = []

        @block.gpsimd
        def _(g, ti=ti):
            s_done.wait(g)
            v_done.wait(g)
            g_done.inc(g.iota(t["ptile"][:], [[1, 1]], base=ti * 128, channel_multiplier=1,
                              allow_small_or_imprecise_dtypes=True))

        def v0(v, ti=ti, cols=cols):
            g_done.wait(v)
            v.tensor_scalar(t["ang"][:], t["ivf"][:], t["ptile"][:, :1], None, op0=ALU.mult)
            v.drain()
            v.tensor_scalar(t["angc"][:], t["ang"][:], 1.0, H1C, op0=ALU.mult, op1=ALU.add)
            v.drain()
            v.tensor_scalar_add(t["angc"][:], t["angc"][:], H2C)
            v.tensor_scalar(t["pw"][:], t["ptile"][:], 1.0, -2048.0, op0=ALU.mult, op1=ALU.add)
            v.drain()
            v.tensor_scalar(t["pw"][:], t["pw"][:], 0.00390625, None, op0=ALU.mult)
            v.drain()
            v.tensor_scalar(t["u2"][:], t["lnb"][:], t["pw"][:, :1], None, op0=ALU.mult)
        steps.append(("v", v0))

        def s0(s, cols=cols):
            s.activation(sb["Txs"][:, cols], t["u2"][:], AF.Exp)
            s.activation(sb["Trxs"][:, cols], t["u2"][:], AF.Exp, scale=-1.0)
        steps.append(("s", s0))

        emit_sin(steps, "ang", sb["Tsin"][:, cols])
        emit_sin(steps, "angc", sb["Tcos"][:, cols])

        for engine, fn in steps:
            if engine == "v":
                @block.vector
                def _(v, fn=fn):
                    s_done.wait(v)
                    fn(v)
                    v_done.inc(v.memset(ctx.sb["barv"][:1, 3:4], 0.0))
            else:
                @block.scalar
                def _(s, fn=fn):
                    v_done.wait(s)
                    fn(s)
                    s_done.inc(s.activation(ctx.sb["barv"][:1, 1:2], ctx.sb["barv"][:1, 1:2],
                                            AF.Copy, scale=0.0))

    tstx = ctx.sem("t0_st")

    @block.sync
    def _(sync):
        s_done.wait(sync)
        v_done.wait(sync)
        for i, nm in enumerate(("Tcos", "Tsin", "Txs", "Trxs")):
            tstx.inc(sync.dma_start(out=ctx.dram["tab_f"][i * 128:(i + 1) * 128, :],
                                    in_=sb[nm][:]), 16)

    if _os.environ.get("K_DBG_TBL") == "1":
        dbg = ctx.sem("tdbg")

        @block.gpsimd
        def _(g):
            s_done.wait(g)
            v_done.wait(g)
            dbg.inc(g.dma_start(out=ctx.dram["qkvF_bf"][0:128, 0:1024], in_=sb["Tcos"][:]), 16)
            dbg.inc(g.dma_start(out=ctx.dram["qkvF_bf"][128:256, 0:1024], in_=sb["Tsin"][:]), 16)
            dbg.inc(g.dma_start(out=ctx.dram["qkvF_bf"][256:384, 0:1024], in_=sb["Txs"][:]), 16)
            dbg.inc(g.dma_start(out=ctx.dram["qkvF_bf"][384:512, 0:1024], in_=sb["Trxs"][:]), 16)
            g.wait_ge(dbg.h, dbg.n)




def emit_p1a(ctx):
    nc, es, block = ctx.nc, ctx.es, ctx.block
    x_in = ctx.dram["x"]
    z_bf = ctx.dram["z_bf"]

    ph = ctx.phase_stack
    xt = [ph.enter_context(nc.sbuf_tensor(f"a_x{i}", [128, D], F32)) for i in range(2)]
    ct = [ph.enter_context(nc.sbuf_tensor(f"a_c{i}", [128, D], F32)) for i in range(2)]
    zt = [ph.enter_context(nc.sbuf_tensor(f"a_z{i}", [128, D], F32)) for i in range(2)]
    dump = ph.enter_context(nc.sbuf_tensor("a_dump", [128, D], F32))
    st = [ph.enter_context(nc.sbuf_tensor(f"a_st{i}", [128, 8], F32)) for i in range(2)]
    ld = Ring(ctx, "hwA", 2)
    cfree = ctx.sem("a_cfree")
    ssd = ctx.sem("a_ssd")
    zs = ctx.sem("a_zs")
    stc = Ring(ctx, "swA", 2)

    @block.sync
    def _(sync):
        for ti in range(NT):
            if ti >= 2:
                cfree.wait(sync, ti - 1)
            ld.inc(sync.dma_start(out=xt[ti % 2][:], in_=x_in[ti * 128:(ti + 1) * 128, :]), ti)

    @block.vector
    def _(v):
        for ti in range(NT):
            x, c, s_ = xt[ti % 2], ct[ti % 2], st[ti % 2]
            ld.wait(v, ti)
            v.reduce_sum(s_[:, 0:1], x[:], axis=AX)
            v.drain()
            v.tensor_scalar(s_[:, 0:1], s_[:, 0:1], 1.0 / D, None, op0=ALU.mult)
            v.drain()
            cfree.inc(v.tensor_scalar(c[:], x[:], s_[:, 0:1], None, op0=ALU.subtract))
            v.drain()
            ssd.wait(v, ti + 1)
            v.tensor_mul(s_[:, 4:5], s_[:, 2:3], s_[:, 3:4])
            v.drain()
            v.tensor_scalar_max(s_[:, 4:5], s_[:, 4:5], 1e-12)
            v.drain()
            v.reciprocal(s_[:, 5:6], s_[:, 4:5])
            v.drain()
            v.tensor_mul(s_[:, 6:7], s_[:, 2:3], s_[:, 5:6])
            v.drain()
            v.tensor_scalar(s_[:, 6:7], s_[:, 6:7], 32.0, None, op0=ALU.mult)
            v.drain()
            if ti >= 2:
                stc.wait(v, ti - 2)
            zs.inc(v.tensor_scalar(zt[ti % 2][:], c[:], s_[:, 6:7], None, op0=ALU.mult))

    @block.scalar
    def _(s):
        for ti in range(NT):
            cfree.wait(s, ti + 1)
            if ti >= 2:
                zs.wait(s, ti - 1)
            c, s_ = ct[ti % 2], st[ti % 2]
            s.activation(dump[:], c[:], AF.Square, accum_out=s_[:, 1:2])
            s.drain()
            s.activation(s_[:, 2:3], s_[:, 1:2], AF.Ln, scale=1.0 / D, bias=1e-5)
            s.drain()
            s.activation(s_[:, 2:3], s_[:, 2:3], AF.Exp, scale=-0.5)
            s.drain()
            s.activation(s_[:, 3:4], s_[:, 1:2], AF.Ln, bias=1e-30)
            s.drain()
            ssd.inc(s.activation(s_[:, 3:4], s_[:, 3:4], AF.Exp, scale=0.5))

    @block.gpsimd
    def _(g):
        for ti in range(NT):
            zs.wait(g, ti + 1)
            stc.inc(g.dma_start(out=z_bf[ti * 128:(ti + 1) * 128, :], in_=zt[ti % 2][:]), ti)


def emit_p1b(ctx):
    nc, block = ctx.nc, ctx.block
    ph = ctx.phase_stack
    z_bf = ctx.dram["z_bf"]
    qkvA_bf = ctx.dram["qkvA_bf"]
    wqkvA = ctx.dram["wqkvA"]
    gammaA = ctx.dram["gammaA"]

    zT = ph.enter_context(nc.sbuf_tensor("b_zT", [128, 8 * NTOK], BF16))
    wq = ph.enter_context(nc.sbuf_tensor("b_wq", [128, 8 * 3072], BF16))
    gam = ph.enter_context(nc.sbuf_tensor("b_gam", [128, 8], F32))
    qv = [ph.enter_context(nc.sbuf_tensor(f"b_qv{i}", [128, 3072], F32)) for i in range(2)]
    qp = ph.enter_context(nc.psum_tensor("b_qp", [128, 3072], F32))

    ld_zt = ctx.sem("b_ldzt")
    ld_w = ctx.sem("b_ldw")
    ld_g = ctx.sem("b_ldg")
    wf = ctx.sem("b_wf")
    mmd = ctx.sem("b_mmd")
    ev = ctx.sem("b_ev")
    stv = Ring(ctx, "swA", 2)

    @block.sync
    def _(sync):
        ld_g.inc(sync.dma_start(out=gam[:], in_=AP(gammaA.tensor, 0, [[1, 128], [128, 8]]), allow_slow_non_contiguous=True), 16)
        for cc in range(8):
            ld_zt.inc(sync.dma_start_transpose(
                zT[:, cc * NTOK:(cc + 1) * NTOK],
                z_bf[:, cc * 128:(cc + 1) * 128]), 16)

    @block.gpsimd
    def _(g):
        for cc in range(8):
            ld_w.inc(g.dma_start(out=wq[:, cc * 3072:(cc + 1) * 3072],
                                 in_=wqkvA[cc * 128:(cc + 1) * 128, :]), 16)
        for ti in range(NT):
            ev.wait(g, 2 * (ti + 1))
            stv.inc(g.dma_start(out=qkvA_bf[ti * 128:(ti + 1) * 128, :], in_=qv[ti % 2][:]), ti)

    @block.vector
    def _(v):
        ld_g.wait(v)
        for cc in range(8):
            ld_w.wait(v, 16 * (cc + 1))
            sl = wq[:, cc * 3072:(cc + 1) * 3072]
            wf.inc(v.tensor_scalar(sl, sl, gam[:, cc:cc + 1], None, op0=ALU.mult))
        for ti in range(NT):
            mmd.wait(v, ti + 1)
            if ti >= 2:
                stv.wait(v, ti - 2)
            ev.inc(v.tensor_copy(qv[ti % 2][:, 0:1536], qp[:, 0:1536]))

    @block.scalar
    def _(s):
        for ti in range(NT):
            mmd.wait(s, ti + 1)
            if ti >= 2:
                stv.wait(s, ti - 2)
            ev.inc(s.copy(qv[ti % 2][:, 1536:3072], qp[:, 1536:3072]))

    @block.tensor
    def _(t):
        ld_zt.wait(t)
        wf.wait(t)
        for ti in range(NT):
            if ti >= 1:
                ev.wait(t, 2 * ti)
            for cc in range(8):
                lhsT = zT[:, cc * NTOK + ti * 128: cc * NTOK + (ti + 1) * 128]
                for g6 in range(6):
                    instr = t.matmul(qp[:, g6 * 512:(g6 + 1) * 512],
                                     lhsT,
                                     wq[:, cc * 3072 + g6 * 512: cc * 3072 + (g6 + 1) * 512],
                                     start=(cc == 0), stop=(cc == 7))
            mmd.inc(instr)


def emit_p1c(ctx):
    nc, block = ctx.nc, ctx.block
    ph = ctx.phase_stack
    sb = ctx.sb
    qkvA_bf = ctx.dram["qkvA_bf"]
    x_in = ctx.dram["x"]
    x1_f = ctx.dram["x1_f"]
    woutT = ctx.dram["woutT"]

    KT = ph.enter_context(nc.sbuf_tensor("c_KT", [128, 8 * NTOK], BF16))
    V = ph.enter_context(nc.sbuf_tensor("c_V", [128, NT * D], BF16))
    WO = ph.enter_context(nc.sbuf_tensor("c_WO", [128, 8 * D], BF16))
    QT = [ph.enter_context(nc.sbuf_tensor(f"c_QT{i}", [128, D], BF16)) for i in range(2)]
    SSb = ph.enter_context(nc.sbuf_tensor("c_SSb", [128, NTOK], F32))
    Pb = [ph.enter_context(nc.sbuf_tensor(f"c_Pb{i}", [128, NTOK], BF16)) for i in range(3)]
    PTsb = [ph.enter_context(nc.sbuf_tensor(f"c_PT{i}", [128, 512], BF16)) for i in range(2)]
    OTsb = ph.enter_context(nc.sbuf_tensor("c_OT", [128, D], BF16))
    sst = [ph.enter_context(nc.sbuf_tensor(f"c_sst{i}", [128, 16], F32)) for i in range(3)]
    lns = [ph.enter_context(nc.sbuf_tensor(f"c_lns{i}", [128, 8], F32)) for i in range(2)]
    xt = [ph.enter_context(nc.sbuf_tensor("c_x0", [128, D], F32))] * 2
    x1t = [ph.enter_context(nc.sbuf_tensor("c_x10", [128, D], F32))] * 2
    Sp = [ph.enter_context(nc.psum_tensor(f"c_Sp{i}", [128, 512], F32)) for i in range(2)]
    ptp = [ph.enter_context(nc.psum_tensor(f"c_ptp{i}", [128, 1024], BF16)) for i in range(2)]
    otp = [ph.enter_context(nc.psum_tensor(f"c_otp{i}", [128, 512], F32)) for i in range(2)]
    opp = ph.enter_context(nc.psum_tensor("c_opp", [128, 1024], F32))

    units = [(qt, h) for qt in range(NT) for h in range(8)]
    NU = len(units)

    def nch(qt):
        return (128 * (qt + 1) + 511) // 512

    def ngrp(qt):
        return (qt + 1 + 3) // 4

    cum_ch = [0]
    cum_gr = [0]
    for (qt, h) in units:
        cum_ch.append(cum_ch[-1] + nch(qt))
        cum_gr.append(cum_gr[-1] + ngrp(qt))

    kld = ctx.sem("c_kld")
    vld = ctx.sem("c_vld")
    wld = ctx.sem("c_wld")
    qld = Ring(ctx, "hwA", 2)
    xld = Ring(ctx, "hwB", 2)
    smm = ctx.sem("c_smm")
    sed = ctx.sem("c_sed")
    vmc = ctx.sem("c_vmc")
    vms = ctx.sem("c_vms")
    vmx = ctx.sem("c_vmx")
    expd = ctx.sem("c_expd")
    vrz = ctx.sem("c_vrz")
    pnd = ctx.sem("c_pnd")
    ttd = ctx.sem("c_ttd")
    evdv = ctx.sem("c_evdv")
    evds = ctx.sem("c_evds")
    pvd = ctx.sem("c_pvd")
    otd = ctx.sem("c_otd")
    opd = ctx.sem("c_opd")
    vxr = ctx.sem("c_vxr")
    stx = Ring(ctx, "hwC", 2)
    qsc = ctx.sem("c_qsc")
    vmu = ctx.sem("c_vmu")
    sq2 = ctx.sem("c_sq2")
    vy0 = ctx.sem("c_vy0")
    lnsf = Ring(ctx, "hwD", 2)
    lnsg = Ring(ctx, "swB", 2)
    yA_f = ctx.dram["yA_f"]
    yA_bf = ctx.dram["yA_bf"]

    @block.sync
    def _(sync):
        for hh in range(8):
            kld.inc(sync.dma_start_transpose(
                KT[:, hh * NTOK:(hh + 1) * NTOK],
                qkvA_bf[:, D + hh * 128: D + (hh + 1) * 128]), 16)
        vld.inc(sync.dma_start(
            out=AP(V, 0, [[NT * D, 128], [D, NT], [1, D]]),
            in_=AP(qkvA_bf.tensor, 2 * D, [[3 * D, 128], [128 * 3 * D, NT], [1, D]])), 16)
        for qt in range(NT):
            if qt >= 2:
                smm.wait(sync, cum_ch[(qt - 1) * 8])
            for hh in range(8):
                qld.inc(sync.dma_start_transpose(
                    QT[qt % 2][:, hh * 128:(hh + 1) * 128],
                    qkvA_bf[qt * 128:(qt + 1) * 128, hh * 128:(hh + 1) * 128]), qt)
            if qt >= 1:
                vy0.wait(sync, qt)
                lnsf.inc(sync.dma_start(out=yA_f[(qt - 1) * 128:qt * 128, :],
                                        in_=xt[(qt - 1) % 2][:]), qt - 1)
                lnsf.wait(sync, qt - 1)
                lnsg.wait(sync, qt - 1)
                vxr.wait(sync, qt)
            xld.inc(sync.dma_start(out=xt[qt % 2][:], in_=x_in[qt * 128:(qt + 1) * 128, :]), qt)
            if qt >= 1:
                vxr.wait(sync, qt)
                stx.inc(sync.dma_start(out=x1_f[(qt - 1) * 128:qt * 128, :],
                                       in_=x1t[(qt - 1) % 2][:]), qt - 1)
        vxr.wait(sync, NT)
        stx.inc(sync.dma_start(out=x1_f[(NT - 1) * 128:NT * 128, :], in_=x1t[(NT - 1) % 2][:]), NT - 1)
        vy0.wait(sync, NT)
        lnsf.inc(sync.dma_start(out=yA_f[(NT - 1) * 128:NT * 128, :], in_=xt[(NT - 1) % 2][:]), NT - 1)

    @block.gpsimd
    def _(g):
        for hh in range(8):
            wld.inc(g.dma_start(out=WO[:, hh * D:(hh + 1) * D],
                                in_=woutT[hh * 128:(hh + 1) * 128, :]), 16)
        for qt in range(NT):
            vy0.wait(g, qt + 1)
            lnsg.inc(g.dma_start(out=yA_bf[qt * 128:(qt + 1) * 128, :], in_=xt[qt % 2][:]), qt)

    @block.tensor
    def _(t):
        kld.wait(t)
        vld.wait(t)
        wld.wait(t)

        def t_smm(u):
            qt, h = units[u]
            klen = 128 * (qt + 1)
            if h == 0:
                qld.wait(t, qt, k=128)
                qsc.wait(t, qt + 1)
            for c in range(nch(qt)):
                w = min(512, klen - c * 512)
                gidx = cum_ch[u] + c
                if gidx >= 2:
                    sed.wait(t, gidx - 1)
                instr = t.matmul(Sp[gidx % 2][:, :w],
                                 QT[qt % 2][:, h * 128:(h + 1) * 128],
                                 KT[:, h * NTOK + c * 512: h * NTOK + c * 512 + w],
                                 start=True, stop=True)
                smm.inc(instr)

        def t_back(u):
            qt, h = units[u]
            ng_ = ngrp(qt)
            pnd.wait(t, u + 1)
            if u >= 2:
                otd.wait(t, u - 1)

            def emit_T(grp):
                c0 = grp * 4
                cn = min(4, (qt + 1) - c0)
                gg = cum_gr[u] + grp
                if gg >= 2:
                    (evdv if gg % 2 == 0 else evds).wait(t, gg // 2)
                for ci in range(cn):
                    c = c0 + ci
                    instr = t.transpose(ptp[gg % 2][:, ci * 128:(ci + 1) * 128],
                                        Pb[u % 3][:, c * 128:(c + 1) * 128], sb["ident"][:])
                ttd.inc(instr)

            def emit_PV(grp):
                c0 = grp * 4
                cn = min(4, (qt + 1) - c0)
                gg = cum_gr[u] + grp
                (evdv if gg % 2 == 0 else evds).wait(t, gg // 2 + 1)
                for ci in range(cn):
                    c = c0 + ci
                    instr = t.matmul(otp[u % 2][:, 0:128],
                                     V[:, c * D + h * 128: c * D + (h + 1) * 128],
                                     PTsb[gg % 2][:, ci * 128:(ci + 1) * 128],
                                     start=(c == 0), stop=(c == qt))
                pvd.inc(instr)

            emit_T(0)
            for grp in range(1, ng_):
                emit_T(grp)
                emit_PV(grp - 1)
            emit_PV(ng_ - 1)
            if h == 7:
                otd.wait(t, u + 1)
                if qt >= 1:
                    vxr.wait(t, qt)
                for hh in range(8):
                    for g2 in range(2):
                        instr = t.matmul(opp[:, g2 * 512:(g2 + 1) * 512],
                                         OTsb[:, hh * 128:(hh + 1) * 128],
                                         WO[:, hh * D + g2 * 512: hh * D + (g2 + 1) * 512],
                                         start=(hh == 0), stop=(hh == 7))
                opd.inc(instr)

        for u in range(NU + 2):
            if u < NU:
                t_smm(u)
            if u >= 2:
                t_back(u - 2)

    @block.scalar
    def _(s):
        def s_evac(u):
            qt, h = units[u]
            klen = 128 * (qt + 1)
            if u >= 1:
                vms.wait(s, u)
                vmc.wait(s, cum_ch[u])
            for c in range(nch(qt)):
                w = min(512, klen - c * 512)
                gidx = cum_ch[u] + c
                smm.wait(s, gidx + 1)
                sed.inc(s.copy(SSb[:, c * 512:c * 512 + w], Sp[gidx % 2][:, :w]))

        def s_exp(u):
            qt, h = units[u]
            klen = 128 * (qt + 1)
            vmx.wait(s, u + 1)
            vms.wait(s, u + 1)
            if u >= 3:
                ttd.wait(s, cum_gr[u - 2])
                pnd.wait(s, u - 2)
                vrz.wait(s, u - 2)
            expd.inc(s.activation(Pb[u % 3][:, :klen], SSb[:, :klen], AF.Exp,
                                  bias=sst[u % 3][:, 0:1], accum_out=sst[u % 3][:, 1:2]))

        def s_tail(u):
            qt, h = units[u]
            ng_ = ngrp(qt)
            for grp in range(ng_):
                gg = cum_gr[u] + grp
                if gg % 2 != 1:
                    continue
                cn = min(4, (qt + 1) - grp * 4)
                ttd.wait(s, gg + 1)
                if gg >= 2:
                    pvd.wait(s, gg - 1)
                evds.inc(s.copy(PTsb[1][:, :cn * 128], ptp[1][:, :cn * 128]))

        def s_ln(qt):
            l_ = lns[qt % 2]
            vxr.wait(s, qt + 1)
            vmu.wait(s, qt + 1)
            s.activation(xt[qt % 2][:], x1t[qt % 2][:], AF.Square, accum_out=l_[:, 2:3])
            s.drain()
            s.activation(l_[:, 4:5], l_[:, 2:3], AF.Ln, scale=1.0 / D, bias=l_[:, 3:4])
            s.drain()
            sq2.inc(s.activation(l_[:, 4:5], l_[:, 4:5], AF.Exp, scale=-0.5))

        for u in range(NU + 2):
            if u >= 1 and u - 1 < NU:
                s_exp(u - 1)
            if u < NU:
                s_evac(u)
            if u >= 10 and (u - 10) % 8 == 0:
                s_ln((u - 10) // 8)
            if u >= 2:
                s_tail(u - 2)
        s_ln(NT - 1)

    @block.vector
    def _(v):
        def v_s1(u):
            qt, h = units[u]
            klen = 128 * (qt + 1)
            nc_ = nch(qt)
            sp_ = sst[u % 3]
            if h == 0:
                qld.wait(v, qt, k=128)
                qsc.inc(v.tensor_scalar(QT[qt % 2][:], QT[qt % 2][:], SCALE_ATTN, None,
                                        op0=ALU.mult))
            for c in range(nc_):
                w = min(512, klen - c * 512)
                gidx = cum_ch[u] + c
                sed.wait(v, gidx + 1)
                vmc.inc(v.reduce_max(sp_[:, 8 + c:9 + c], SSb[:, c * 512:c * 512 + w], axis=AX))
            v.drain()
            v.reduce_max(sp_[:, 3:4], sp_[:, 8:8 + nc_], axis=AX)
            v.drain()
            if u >= 3:
                expd.wait(v, u - 2)
            vmx.inc(v.tensor_scalar(sp_[:, 0:1], sp_[:, 3:4], -1.0, None, op0=ALU.mult))
            if u >= 1:
                expd.wait(v, u)
            vms.inc(v.tensor_add(SSb[:, klen - 128:klen], SSb[:, klen - 128:klen],
                                 sb["mdiag"][:]))

        def v_s2(u):
            qt, h = units[u]
            klen = 128 * (qt + 1)
            sp_ = sst[u % 3]
            expd.wait(v, u + 1)
            if u >= 3:
                pnd.wait(v, u - 2)
            vrz.inc(v.reciprocal(sp_[:, 2:3], sp_[:, 1:2]))
            v.drain()
            pnd.inc(v.tensor_scalar(Pb[u % 3][:, :klen], Pb[u % 3][:, :klen],
                                    sp_[:, 2:3], None, op0=ALU.mult))

        def v_s3(u):
            qt, h = units[u]
            ng_ = ngrp(qt)
            for grp in range(ng_):
                gg = cum_gr[u] + grp
                if gg % 2 != 0:
                    continue
                cn = min(4, (qt + 1) - grp * 4)
                ttd.wait(v, gg + 1)
                if gg >= 2:
                    pvd.wait(v, gg - 1)
                evdv.inc(v.tensor_copy(PTsb[0][:, :cn * 128], ptp[0][:, :cn * 128]))
            pvd.wait(v, cum_gr[u + 1])
            if qt >= 1:
                opd.wait(v, qt)
            otd.inc(v.tensor_copy(OTsb[:, h * 128:(h + 1) * 128], otp[u % 2][:, 0:128]))
            if h == 7:
                opd.wait(v, qt + 1)
                xld.wait(v, qt)
                if qt >= 1:
                    stx.wait(v, qt - 1)
                    sq2.wait(v, qt)
                vxr.inc(v.tensor_add(x1t[qt % 2][:], opp[:], xt[qt % 2][:]))
                l_ = lns[qt % 2]
                v.reduce_sum(l_[:, 0:1], x1t[qt % 2][:], axis=AX)
                v.drain()
                v.tensor_scalar(l_[:, 1:2], l_[:, 0:1], 1.0 / D, None, op0=ALU.mult)
                v.drain()
                v.tensor_mul(l_[:, 3:4], l_[:, 1:2], l_[:, 1:2])
                v.drain()
                vmu.inc(v.tensor_scalar(l_[:, 3:4], l_[:, 3:4], -1.0, 1e-5,
                                        op0=ALU.mult, op1=ALU.add))
            if h == 1 and qt >= 1:
                l_ = lns[(qt - 1) % 2]
                sq2.wait(v, qt)
                vy0.inc(v.tensor_scalar(xt[(qt - 1) % 2][:], x1t[(qt - 1) % 2][:],
                                        l_[:, 1:2], l_[:, 4:5],
                                        op0=ALU.subtract, op1=ALU.mult))

        for u in range(NU + 2):
            if u < NU:
                v_s1(u)
            if 1 <= u and u - 1 < NU:
                v_s2(u - 1)
            if u >= 2:
                v_s3(u - 2)
        l_ = lns[(NT - 1) % 2]
        sq2.wait(v, NT)
        vy0.inc(v.tensor_scalar(xt[(NT - 1) % 2][:], x1t[(NT - 1) % 2][:],
                                l_[:, 1:2], l_[:, 4:5],
                                op0=ALU.subtract, op1=ALU.mult))

def emit_p2pre(ctx):
    pass



def emit_ln_stream(ctx, tag, src_f, dst_f, dst_bf):
    nc, block = ctx.nc, ctx.block
    ph = ctx.phase_stack
    xt = [ph.enter_context(nc.sbuf_tensor(f"{tag}_x{i}", [128, D], F32)) for i in range(2)]
    yt = [ph.enter_context(nc.sbuf_tensor(f"{tag}_y{i}", [128, D], F32)) for i in range(2)]
    dump = ph.enter_context(nc.sbuf_tensor(f"{tag}_dump", [128, D], F32))
    st = [ph.enter_context(nc.sbuf_tensor(f"{tag}_st{i}", [128, 4], F32)) for i in range(2)]
    ld = Ring(ctx, "hwA", 2)
    cfree = ctx.sem(tag + "_cfree")
    ssd = ctx.sem(tag + "_ssd")
    yd = ctx.sem(tag + "_yd")
    stc = Ring(ctx, "hwB", 2)
    stcg = Ring(ctx, "swA", 2)

    @block.sync
    def _(sync):
        for ti in range(NT):
            if ti >= 2:
                yd.wait(sync, ti - 1)
            ld.inc(sync.dma_start(out=xt[ti % 2][:], in_=src_f[ti * 128:(ti + 1) * 128, :]), ti)
            if ti >= 1:
                yd.wait(sync, ti)
                stc.inc(sync.dma_start(out=dst_f[(ti - 1) * 128:ti * 128, :],
                                       in_=yt[(ti - 1) % 2][:]), ti - 1)
        yd.wait(sync, NT)
        stc.inc(sync.dma_start(out=dst_f[(NT - 1) * 128:NT * 128, :], in_=yt[(NT - 1) % 2][:]), NT - 1)

    @block.vector
    def _(v):
        for ti in range(NT):
            x, s_ = xt[ti % 2], st[ti % 2]
            ld.wait(v, ti)
            v.reduce_sum(s_[:, 0:1], x[:], axis=AX)
            v.drain()
            v.tensor_scalar(s_[:, 0:1], s_[:, 0:1], 1.0 / D, None, op0=ALU.mult)
            v.drain()
            cfree.inc(v.tensor_scalar(x[:], x[:], s_[:, 0:1], None, op0=ALU.subtract))
            v.drain()
            ssd.wait(v, ti + 1)
            if ti >= 2:
                stc.wait(v, ti - 2)
                stcg.wait(v, ti - 2)
            yd.inc(v.tensor_scalar(yt[ti % 2][:], x[:], s_[:, 2:3], None, op0=ALU.mult))

    @block.scalar
    def _(s):
        for ti in range(NT):
            cfree.wait(s, ti + 1)
            if ti >= 2:
                yd.wait(s, ti - 1)
            c, s_ = xt[ti % 2], st[ti % 2]
            s.activation(dump[:], c[:], AF.Square, accum_out=s_[:, 1:2])
            s.drain()
            s.activation(s_[:, 2:3], s_[:, 1:2], AF.Ln, scale=1.0 / D, bias=1e-5)
            s.drain()
            ssd.inc(s.activation(s_[:, 2:3], s_[:, 2:3], AF.Exp, scale=-0.5))

    @block.gpsimd
    def _(g):
        for ti in range(NT):
            yd.wait(g, ti + 1)
            stcg.inc(g.dma_start(out=dst_bf[ti * 128:(ti + 1) * 128, :], in_=yt[ti % 2][:]), ti)


def emit_p2a(ctx, l):
    nc, block, sb = ctx.nc, ctx.block, ctx.sb
    ph = ctx.phase_stack
    yin_f = ctx.dram["yA_f"]
    yin_bf = ctx.dram["yA_bf"]
    ymid_f = ctx.dram["yB_f"]
    qr_bf = ctx.dram["qr_bf"]
    kr_bf = ctx.dram["kr_bf"]
    P = "g"

    V = ph.enter_context(nc.sbuf_tensor(P + str(l) + "V", [128, NT * 512], BF16))
    for nm in ("Tcos", "Tsin", "Txs", "Trxs"):
        sb[nm] = ph.enter_context(nc.sbuf_tensor(P + str(l) + nm, [128, NT * 32], F32))
    tabl = ctx.psem(P + "tabl")

    @block.sync
    def _(sync):
        for i, nm in enumerate(("Tcos", "Tsin", "Txs", "Trxs")):
            tabl.inc(sync.dma_start(out=sb[nm][:],
                                    in_=ctx.dram["tab_f"][i * 128:(i + 1) * 128, :]), 16)
    s1 = ExitStack()
    syT = s1.enter_context(nc.sbuf_tensor(P + str(l) + "syT", [128, 8 * NTOK], BF16))
    wqf = s1.enter_context(nc.sbuf_tensor(P + str(l) + "wqf", [128, 8 * 1536], BF16))
    tmpT = s1.enter_context(nc.sbuf_tensor(P + str(l) + "tmpT", [128, NTOK], BF16))
    qsrow = s1.enter_context(nc.sbuf_tensor(P + str(l) + "qsrow", [1, 64], F32))
    ksrow = s1.enter_context(nc.sbuf_tensor(P + str(l) + "ksrow", [1, 64], F32))
    qsrep = s1.enter_context(nc.sbuf_tensor(P + str(l) + "qsrep", [128, 64], F32))
    ksrep = s1.enter_context(nc.sbuf_tensor(P + str(l) + "ksrep", [128, 64], F32))
    qsB = s1.enter_context(nc.sbuf_tensor(P + str(l) + "qsB", [128, 64], F32))
    ksB = s1.enter_context(nc.sbuf_tensor(P + str(l) + "ksB", [128, 64], F32))
    tabs = s1.enter_context(nc.sbuf_tensor(P + str(l) + "tabs", [128, 4 * 64], F32))
    tabu = s1.enter_context(nc.sbuf_tensor(P + str(l) + "tabu", [128, 32], F32))
    qkt = [s1.enter_context(nc.sbuf_tensor(P + str(l) + f"qkt{i}", [128, 1024], F32)) for i in range(2)]
    qro = [s1.enter_context(nc.sbuf_tensor(P + str(l) + f"qro{i}", [128, 1024], F32)) for i in range(2)]
    qtmp = s1.enter_context(nc.sbuf_tensor(P + str(l) + "qtmp", [128, 512], F32))
    rnq = s1.enter_context(nc.sbuf_tensor(P + str(l) + "rnq", [128, 16], F32))
    pq = s1.enter_context(nc.psum_tensor(P + str(l) + "pq", [128, 1536], F32))
    prep = s1.enter_context(nc.psum_tensor(P + str(l) + "prep", [128, 512], F32))

    syld = ctx.psem(P + "syld")
    tld = Ring(ctx, "hwD", 1)
    shf = ctx.psem(P + "shf")
    wql = ctx.psem(P + "wql")
    qsl = ctx.psem(P + "qsl")
    repd = ctx.psem(P + "repd")
    repe = ctx.psem(P + "repe")
    mmd = ctx.psem(P + "mmd")
    qev = ctx.psem(P + "qev")
    vev = ctx.psem(P + "vev")
    vq1 = ctx.psem(P + "vq1")
    sq1 = ctx.psem(P + "sq1")
    vq2 = ctx.psem(P + "vq2")
    qst = Ring(ctx, "swA", 2)

    @block.sync
    def _(sync):
        qsl.inc(sync.dma_start(out=qsrow[:], in_=AP(ctx.dram[f"qs{l}"].tensor, 0, [[64, 1], [1, 64]])), 16)
        qsl.inc(sync.dma_start(out=ksrow[:], in_=AP(ctx.dram[f"ks{l}"].tensor, 0, [[64, 1], [1, 64]])), 16)
        for cc in range(4):
            syld.inc(sync.dma_start_transpose(
                syT[:, cc * NTOK:(cc + 1) * NTOK],
                yin_bf[:, cc * 128:(cc + 1) * 128]), 16)
        for cc in range(4, 8):
            if cc > 4:
                shf.wait(sync, cc - 4)
            tld.inc(sync.dma_start_transpose(
                tmpT[:], yin_bf[:, cc * 128:(cc + 1) * 128]), cc - 4)

    @block.vector
    def _(v):
        for i, cc in enumerate(range(4, 8)):
            tld.wait(v, i)
            v.memset(syT[:, cc * NTOK: cc * NTOK + 1], 0.0)
            v.drain()
            shf.inc(v.tensor_copy(syT[:, cc * NTOK + 1: (cc + 1) * NTOK], tmpT[:, 0:NTOK - 1]))

    @block.gpsimd
    def _(g):
        for cc in range(8):
            wql.inc(g.dma_start(out=wqf[:, cc * 1536:(cc + 1) * 1536],
                                in_=ctx.dram[f"wqkvF{l}"][cc * 128:(cc + 1) * 128, :]), 16)

    @block.tensor
    def _(t):
        qsl.wait(t)
        t.matmul(prep[:, 0:64], sb["ones1"][:], qsrow[:], start=True, stop=True)
        repd.inc(t.matmul(prep[:, 64:128], sb["ones1"][:], ksrow[:], start=True, stop=True))

    @block.vector
    def _(v):
        repd.wait(v)
        v.tensor_scalar(qsrep[:], prep[:, 0:64], 8.0, None, op0=ALU.mult)
        v.tensor_copy(ksrep[:], prep[:, 64:128])
        v.drain()
        v.tensor_scalar(qsB[:, 0:32], qsrep[:, 32:64], -1.0, None, op0=ALU.mult)
        v.tensor_copy(qsB[:, 32:64], qsrep[:, 0:32])
        v.tensor_scalar(ksB[:, 0:32], ksrep[:, 32:64], -1.0, None, op0=ALU.mult)
        repe.inc(v.tensor_copy(ksB[:, 32:64], ksrep[:, 0:32]))

    @block.tensor
    def _(t):
        syld.wait(t)
        shf.wait(t)
        wql.wait(t)
        for ti in range(NT):
            if ti >= 1:
                qev.wait(t, ti)
                vev.wait(t, ti)
            for cc in range(8):
                lhsT = syT[:, cc * NTOK + ti * 128: cc * NTOK + (ti + 1) * 128]
                for g3 in range(3):
                    instr = t.matmul(pq[:, g3 * 512:(g3 + 1) * 512], lhsT,
                                     wqf[:, cc * 1536 + g3 * 512: cc * 1536 + (g3 + 1) * 512],
                                     start=(cc == 0), stop=(cc == 7))
            mmd.inc(instr)

    @block.scalar
    def _(s):
        for ti in range(NT):
            mmd.wait(s, ti + 1)
            vev.inc(s.copy(V[:, ti * 512:(ti + 1) * 512], pq[:, 1024:1536]))
            vq1.wait(s, ti + 1)
            s.activation(rnq[:, 0:8], rnq[:, 0:8], AF.Ln, bias=1e-30)
            s.activation(rnq[:, 8:16], rnq[:, 8:16], AF.Ln, bias=1e-30)
            s.drain()
            s.activation(rnq[:, 0:8], rnq[:, 0:8], AF.Exp, scale=-0.5)
            sq1.inc(s.activation(rnq[:, 8:16], rnq[:, 8:16], AF.Exp, scale=-0.5))

    @block.vector
    def _(v):
        repe.wait(v)
        tabl.wait(v)
        for ti in range(NT):
            pr = ti % 2
            tc = slice(ti * 32, (ti + 1) * 32)
            mmd.wait(v, ti + 1)
            if ti >= 2:
                qst.wait(v, ti - 2, k=32)
            qev.inc(v.tensor_copy(qkt[pr][:], pq[:, 0:1024]))
            v.drain()
            v.tensor_mul(qtmp[:], qkt[pr][:, 0:512], qkt[pr][:, 0:512])
            v.drain()
            v.tensor_reduce(rnq[:, 0:8], AP(qtmp, 0, [[512, 128], [64, 8], [1, 64]]),
                            axis=AX, op=ALU.add)
            v.drain()
            v.tensor_mul(qtmp[:], qkt[pr][:, 512:1024], qkt[pr][:, 512:1024])
            v.drain()
            vq1.inc(v.tensor_reduce(rnq[:, 8:16], AP(qtmp, 0, [[512, 128], [64, 8], [1, 64]]),
                                    axis=AX, op=ALU.add))
            v.tensor_mul(tabu[:], sb["Tcos"][:, tc], sb["Txs"][:, tc])
            v.drain()
            v.tensor_mul(tabs[:, 0:32], tabu[:], qsrep[:, 0:32])
            v.tensor_mul(tabs[:, 32:64], tabu[:], qsrep[:, 32:64])
            v.drain()
            v.tensor_mul(tabu[:], sb["Tsin"][:, tc], sb["Txs"][:, tc])
            v.drain()
            v.tensor_mul(tabs[:, 64:96], tabu[:], qsB[:, 0:32])
            v.tensor_mul(tabs[:, 96:128], tabu[:], qsB[:, 32:64])
            v.drain()
            v.tensor_mul(tabu[:], sb["Tcos"][:, tc], sb["Trxs"][:, tc])
            v.drain()
            v.tensor_mul(tabs[:, 128:160], tabu[:], ksrep[:, 0:32])
            v.tensor_mul(tabs[:, 160:192], tabu[:], ksrep[:, 32:64])
            v.drain()
            v.tensor_mul(tabu[:], sb["Tsin"][:, tc], sb["Trxs"][:, tc])
            v.drain()
            v.tensor_mul(tabs[:, 192:224], tabu[:], ksB[:, 0:32])
            v.tensor_mul(tabs[:, 224:256], tabu[:], ksB[:, 32:64])
            sq1.wait(v, ti + 1)
            v.tensor_scalar_min(rnq[:, 0:8], rnq[:, 0:8], 1e12)
            v.tensor_scalar_min(rnq[:, 8:16], rnq[:, 8:16], 1e12)
            v.drain()
            v.tensor_tensor(qkt[pr][:, 0:512], qkt[pr][:, 0:512],
                            AP(rnq, 0, [[16, 128], [1, 8], [0, 64]]), op=ALU.mult)
            v.tensor_tensor(qkt[pr][:, 512:1024], qkt[pr][:, 512:1024],
                            AP(rnq, 8, [[16, 128], [1, 8], [0, 64]]), op=ALU.mult)
            v.drain()
            v.tensor_tensor(qro[pr][:, 0:512], qkt[pr][:, 0:512],
                            AP(tabs, 0, [[256, 128], [0, 8], [1, 64]]), op=ALU.mult)
            v.tensor_tensor(qtmp[:], AP(qkt[pr], 32, [[1024, 128], [64, 8], [-32, 2], [1, 32]]),
                            AP(tabs, 64, [[256, 128], [0, 8], [1, 64]]), op=ALU.mult)
            v.drain()
            v.tensor_add(qro[pr][:, 0:512], qro[pr][:, 0:512], qtmp[:])
            v.tensor_tensor(qro[pr][:, 512:1024], qkt[pr][:, 512:1024],
                            AP(tabs, 128, [[256, 128], [0, 8], [1, 64]]), op=ALU.mult)
            v.drain()
            v.tensor_tensor(qtmp[:], AP(qkt[pr], 544, [[1024, 128], [64, 8], [-32, 2], [1, 32]]),
                            AP(tabs, 192, [[256, 128], [0, 8], [1, 64]]), op=ALU.mult)
            v.drain()
            vq2.inc(v.tensor_add(qro[pr][:, 512:1024], qro[pr][:, 512:1024], qtmp[:]))

    @block.gpsimd
    def _(g):
        for ti in range(NT):
            vq2.wait(g, ti + 1)
            qst.inc(g.dma_start(out=qr_bf[ti * 128:(ti + 1) * 128, :],
                                in_=qro[ti % 2][:, 0:512]), ti)
            qst.inc(g.dma_start(out=kr_bf[ti * 128:(ti + 1) * 128, :],
                                in_=qro[ti % 2][:, 512:1024]), ti)


    s1.close()

    if _os.environ.get("K_NO_A3") == "1":
        return
    KT2 = ph.enter_context(nc.sbuf_tensor(P + str(l) + "KT2", [128, 4 * NTOK], BF16))
    QT2 = ph.enter_context(nc.sbuf_tensor(P + str(l) + "QT2", [128, 4 * NTOK], BF16))
    wo2 = ph.enter_context(nc.sbuf_tensor(P + str(l) + "wo2", [64, 8 * D], BF16))
    SSl = ph.enter_context(nc.sbuf_tensor(P + str(l) + "SSl", [128, 640], F32))
    Pl = [ph.enter_context(nc.sbuf_tensor(P + str(l) + f"Pl{i}", [128, 640], BF16)) for i in range(3)]
    PTl = [ph.enter_context(nc.sbuf_tensor(P + str(l) + f"PTl{i}", [128, 640], BF16)) for i in range(2)]
    OTl = ph.enter_context(nc.sbuf_tensor(P + str(l) + "OTl", [64, 8 * 128], BF16))
    sstl = [ph.enter_context(nc.sbuf_tensor(P + str(l) + f"sstl{i}", [128, 8], F32)) for i in range(3)]
    yint = [ph.enter_context(nc.sbuf_tensor(P + str(l) + f"yin{i}", [128, D], F32)) for i in range(2)]
    y1t = [ph.enter_context(nc.sbuf_tensor(P + str(l) + f"y1t{i}", [128, D], F32)) for i in range(2)]
    Spl = [ph.enter_context(nc.psum_tensor(P + str(l) + f"Spl{i}", [128, 512], F32)) for i in range(2)]
    ptpl = [ph.enter_context(nc.psum_tensor(P + str(l) + f"ptpl{i}", [128, 1024], BF16)) for i in range(2)]
    otl = [ph.enter_context(nc.psum_tensor(P + str(l) + f"otl{i}", [128, 512], F32)) for i in range(2)]
    oppl = ph.enter_context(nc.psum_tensor(P + str(l) + "oppl", [128, 1024], F32))

    k2l = ctx.psem(P + "k2l")
    w2l = ctx.psem(P + "w2l")
    yld = Ring(ctx, "hwB", 2)
    lsm = ctx.psem(P + "lsm")
    lmc = ctx.psem(P + "lmc")
    lms = ctx.psem(P + "lms")
    lse = ctx.psem(P + "lse")
    lvm = ctx.psem(P + "lvm")
    lex = ctx.psem(P + "lex")
    lrz = ctx.psem(P + "lrz")
    lpn = ctx.psem(P + "lpn")
    ltt = ctx.psem(P + "ltt")
    lev = ctx.psem(P + "lev")
    lpv = ctx.psem(P + "lpv")
    lot = ctx.psem(P + "lot")
    lop = ctx.psem(P + "lop")
    lvx = ctx.psem(P + "lvx")
    lst = Ring(ctx, "hwC", 2)

    def ktile0(qt):
        return max(0, qt - 4)

    NU_l = NT * 8
    lunits = [divmod(u, 8) for u in range(NU_l)]

    def lchw(qt):
        klen = (qt - ktile0(qt) + 1) * 128
        return [512, klen - 512] if klen > 512 else [klen]

    cum_lch = [0]
    for (qt, h) in lunits:
        cum_lch.append(cum_lch[-1] + len(lchw(qt)))

    @block.sync
    def _(sync):
        qst.wait_all_issued(sync)
        for p in range(4):
            k2l.inc(sync.dma_start_transpose(
                KT2[:, p * NTOK:(p + 1) * NTOK], kr_bf[:, p * 128:(p + 1) * 128]), 16)
            k2l.inc(sync.dma_start_transpose(
                QT2[:, p * NTOK:(p + 1) * NTOK], qr_bf[:, p * 128:(p + 1) * 128]), 16)
        for qt in range(NT):
            if qt >= 2:
                lvx.wait(sync, qt - 1)
            yld.inc(sync.dma_start(out=yint[qt % 2][:], in_=yin_f[qt * 128:(qt + 1) * 128, :]), qt)
            if qt >= 1:
                lvx.wait(sync, qt)
                lst.inc(sync.dma_start(out=ymid_f[(qt - 1) * 128:qt * 128, :],
                                       in_=y1t[(qt - 1) % 2][:]), qt - 1)
        lvx.wait(sync, NT)
        lst.inc(sync.dma_start(out=ymid_f[(NT - 1) * 128:NT * 128, :], in_=y1t[(NT - 1) % 2][:]), NT - 1)

    @block.gpsimd
    def _(g):
        qst.wait_all_issued(g)
        for hh in range(8):
            w2l.inc(g.dma_start(out=wo2[0:64, hh * D:(hh + 1) * D],
                                in_=ctx.dram[f"wo{l}"][hh * 64:(hh + 1) * 64, :]), 16)

    @block.tensor
    def _(t):
        k2l.wait(t)
        w2l.wait(t)

        def t_smm(u):
            qt, h = lunits[u]
            p, ho = divmod(h, 2)
            rb = 64 * ho
            kt0 = ktile0(qt)
            for c, w in enumerate(lchw(qt)):
                gidx = cum_lch[u] + c
                if gidx >= 2:
                    lse.wait(t, gidx - 1)
                off = p * NTOK + kt0 * 128 + c * 512
                instr = t.matmul(Spl[gidx % 2][:, :w],
                                 QT2[rb:rb + 64, p * NTOK + qt * 128: p * NTOK + (qt + 1) * 128],
                                 KT2[rb:rb + 64, off: off + w],
                                 start=True, stop=True)
                lsm.inc(instr)

        def t_back(u):
            qt, h = lunits[u]
            kt0 = ktile0(qt)
            nkt = qt - kt0 + 1
            lpn.wait(t, u + 1)
            if u >= 1:
                lev.wait(t, u)
            for ci in range(nkt):
                instr = t.transpose(ptpl[u % 2][:, ci * 128:(ci + 1) * 128],
                                    Pl[u % 3][:, ci * 128:(ci + 1) * 128], sb["ident"][:])
            ltt.inc(instr)
            lev.wait(t, u + 1)
            if u >= 2:
                lot.wait(t, u - 1)
            for ci in range(nkt):
                kt = kt0 + ci
                instr = t.matmul(otl[u % 2][0:64, 0:128],
                                 V[:, kt * 512 + h * 64: kt * 512 + (h + 1) * 64],
                                 PTl[u % 2][:, ci * 128:(ci + 1) * 128],
                                 start=(ci == 0), stop=(ci == nkt - 1))
            lpv.inc(instr)
            if h == 7:
                lot.wait(t, u + 1)
                if qt >= 1:
                    lvx.wait(t, qt)
                for hh in range(8):
                    for g2 in range(2):
                        instr = t.matmul(oppl[:, g2 * 512:(g2 + 1) * 512],
                                         OTl[0:64, hh * 128:(hh + 1) * 128],
                                         wo2[0:64, hh * D + g2 * 512: hh * D + (g2 + 1) * 512],
                                         start=(hh == 0), stop=(hh == 7))
                lop.inc(instr)

        for u in range(NU_l + 2):
            if u < NU_l:
                t_smm(u)
            if u >= 2:
                t_back(u - 2)

    @block.scalar
    def _(s):
        def s_exp(u):
            qt, h = lunits[u]
            klen = (qt - ktile0(qt) + 1) * 128
            lvm.wait(s, u + 1)
            lms.wait(s, u + 1)
            if u >= 3:
                ltt.wait(s, u - 2)
                lpn.wait(s, u - 2)
                lrz.wait(s, u - 2)
            lex.inc(s.activation(Pl[u % 3][:, :klen], SSl[:, :klen], AF.Exp,
                                 bias=sstl[u % 3][:, 0:1], accum_out=sstl[u % 3][:, 1:2]))

        def s_evac(u):
            qt, h = lunits[u]
            if u >= 1:
                lms.wait(s, u)
                lmc.wait(s, u)
            for c, w in enumerate(lchw(qt)):
                gidx = cum_lch[u] + c
                lsm.wait(s, gidx + 1)
                lse.inc(s.copy(SSl[:, c * 512:c * 512 + w], Spl[gidx % 2][:, :w]))

        for u in range(NU_l + 2):
            if u >= 1 and u - 1 < NU_l:
                s_exp(u - 1)
            if u < NU_l:
                s_evac(u)

    @block.vector
    def _(v):
        def v_s1(u):
            qt, h = lunits[u]
            klen = (qt - ktile0(qt) + 1) * 128
            sp_ = sstl[u % 3]
            lse.wait(v, cum_lch[u + 1])
            lmc.inc(v.reduce_max(sp_[:, 3:4], SSl[:, :klen], axis=AX))
            v.drain()
            if u >= 3:
                lex.wait(v, u - 2)
            lvm.inc(v.tensor_scalar(sp_[:, 0:1], sp_[:, 3:4], -1.0, None, op0=ALU.mult))
            if u >= 1:
                lex.wait(v, u)
            instr = v.tensor_add(SSl[:, klen - 128:klen], SSl[:, klen - 128:klen], sb["mdiag"][:])
            if qt >= 4:
                instr = v.tensor_add(SSl[:, 0:128], SSl[:, 0:128], sb["mfar"][:])
            lms.inc(instr)

        def v_s2(u):
            qt, h = lunits[u]
            klen = (qt - ktile0(qt) + 1) * 128
            sp_ = sstl[u % 3]
            lex.wait(v, u + 1)
            if u >= 3:
                lpn.wait(v, u - 2)
            lrz.inc(v.reciprocal(sp_[:, 2:3], sp_[:, 1:2]))
            v.drain()
            lpn.inc(v.tensor_scalar(Pl[u % 3][:, :klen], Pl[u % 3][:, :klen],
                                    sp_[:, 2:3], None, op0=ALU.mult))

        def v_s3(u):
            qt, h = lunits[u]
            klen = (qt - ktile0(qt) + 1) * 128
            ltt.wait(v, u + 1)
            if u >= 1:
                lpv.wait(v, u)
            lev.inc(v.tensor_copy(PTl[u % 2][:, :klen], ptpl[u % 2][:, :klen]))
            lpv.wait(v, u + 1)
            if qt >= 1:
                lop.wait(v, qt)
            lot.inc(v.tensor_copy(OTl[0:64, h * 128:(h + 1) * 128], otl[u % 2][0:64, 0:128]))
            if h == 7:
                lop.wait(v, qt + 1)
                yld.wait(v, qt)
                if qt >= 2:
                    lst.wait(v, qt - 2)
                lvx.inc(v.tensor_add(y1t[qt % 2][:], oppl[:], yint[qt % 2][:]))

        for u in range(NU_l + 2):
            if u < NU_l:
                v_s1(u)
            if 1 <= u and u - 1 < NU_l:
                v_s2(u - 1)
            if u >= 2:
                v_s3(u - 2)


def emit_p2b(ctx, l):
    nc, block, sb = ctx.nc, ctx.block, ctx.sb
    ph = ctx.phase_stack
    ymid_f = ctx.dram["yB_f"]
    z2_bf = ctx.dram["z2_bf"]
    P = "m"
    SUP = 4
    TSUP = NTOK // SUP
    TPS = TSUP // 128

    lngrep = ph.enter_context(nc.sbuf_tensor(P + str(l) + "lngrep", [128, D], F32))
    pa = ph.enter_context(nc.psum_tensor(P + str(l) + "pa", [128, 1536], F32))
    pg = ph.enter_context(nc.psum_tensor(P + str(l) + "pg", [128, 1536], F32))
    pd = ph.enter_context(nc.psum_tensor(P + str(l) + "pd", [128, 1024], F32))
    s0 = ExitStack()
    lngrow = s0.enter_context(nc.sbuf_tensor(P + str(l) + "lngrow", [1, D], F32))
    xt = [s0.enter_context(nc.sbuf_tensor(P + str(l) + f"x{i}", [128, D], F32)) for i in range(2)]
    z2t = [s0.enter_context(nc.sbuf_tensor(P + str(l) + f"z2{i}", [128, D], F32)) for i in range(2)]
    dump = s0.enter_context(nc.sbuf_tensor(P + str(l) + "dump", [128, D], F32))
    stt_ = [s0.enter_context(nc.sbuf_tensor(P + str(l) + f"st{i}", [128, 4], F32)) for i in range(2)]

    lngl = ctx.psem(P + "lngl")
    lrep = ctx.psem(P + "lrep")
    lrepe = ctx.psem(P + "lrepe")
    bld = ctx.psem(P + "bld")
    zld = Ring(ctx, "hwA", 2)
    zcf = ctx.psem(P + "zcf")
    zss = ctx.psem(P + "zss")
    zzd = ctx.psem(P + "zzd")
    zst = Ring(ctx, "swA", 2)

    @block.sync
    def _(sync):
        lngl.inc(sync.dma_start(out=lngrow[:], in_=AP(ctx.dram[f"lng{l}"].tensor, 0, [[D, 1], [1, D]])), 16)
        for ti in range(NT):
            if ti >= 2:
                zzd.wait(sync, ti - 1)
            zld.inc(sync.dma_start(out=xt[ti % 2][:, 0:512],
                                   in_=ymid_f[ti * 128:(ti + 1) * 128, 0:512]), ti)
            if ti == 0:
                zld.inc(sync.dma_start(out=xt[0][1:128, 512:1024],
                                       in_=ymid_f[0:127, 512:1024]), ti)
            else:
                zld.inc(sync.dma_start(out=xt[ti % 2][:, 512:1024],
                                       in_=ymid_f[ti * 128 - 1:(ti + 1) * 128 - 1, 512:1024]), ti)

    @block.tensor
    def _(t):
        lngl.wait(t)
        t.matmul(pa[:, 0:512], sb["ones1"][:], lngrow[:, 0:512], start=True, stop=True)
        lrep.inc(t.matmul(pa[:, 512:1024], sb["ones1"][:], lngrow[:, 512:1024], start=True, stop=True))

    @block.vector
    def _(v):
        lrep.wait(v)
        lrepe.inc(v.tensor_copy(lngrep[:], pa[:, 0:1024]))
        v.memset(xt[0][0:1, 512:1024], 0.0)
        v.drain()
        for ti in range(NT):
            x, s_ = xt[ti % 2], stt_[ti % 2]
            zld.wait(v, ti, k=32)
            v.reduce_sum(s_[:, 0:1], x[:], axis=AX)
            v.drain()
            v.tensor_scalar(s_[:, 0:1], s_[:, 0:1], 1.0 / D, None, op0=ALU.mult)
            v.drain()
            zcf.inc(v.tensor_scalar(x[:], x[:], s_[:, 0:1], None, op0=ALU.subtract))
            v.drain()
            zss.wait(v, ti + 1)
            if ti >= 2:
                zst.wait(v, ti - 2)
            v.tensor_scalar(x[:], x[:], s_[:, 2:3], None, op0=ALU.mult)
            v.drain()
            zzd.inc(v.tensor_mul(z2t[ti % 2][:], x[:], lngrep[:]))

    @block.scalar
    def _(s):
        for ti in range(NT):
            zcf.wait(s, ti + 1)
            if ti >= 2:
                zzd.wait(s, ti - 1)
            c, s_ = xt[ti % 2], stt_[ti % 2]
            s.activation(dump[:], c[:], AF.Square, accum_out=s_[:, 1:2])
            s.drain()
            s.activation(s_[:, 2:3], s_[:, 1:2], AF.Ln, scale=1.0 / D, bias=1e-5)
            s.drain()
            zss.inc(s.activation(s_[:, 2:3], s_[:, 2:3], AF.Exp, scale=-0.5))

    @block.gpsimd
    def _(g):
        for ti in range(NT):
            zzd.wait(g, ti + 1)
            zst.inc(g.dma_start(out=z2_bf[ti * 128:(ti + 1) * 128, :], in_=z2t[ti % 2][:]), ti)

    s0.close()

    z2T = ph.enter_context(nc.sbuf_tensor(P + str(l) + "z2T", [128, 8 * TSUP], BF16))
    w1sl = [ph.enter_context(nc.sbuf_tensor(P + str(l) + f"w1sl{i}", [128, 8 * 128], BF16)) for i in range(2)]
    gatedT = ph.enter_context(nc.sbuf_tensor(P + str(l) + "gatedT", [128, NJ * TSUP], BF16))
    w2sb = ph.enter_context(nc.sbuf_tensor(P + str(l) + "w2sb", [128, NJ * D], BF16))
    geluSb = ph.enter_context(nc.sbuf_tensor(P + str(l) + "geluSb", [128, TSUP], F32))
    b1eff = ph.enter_context(nc.sbuf_tensor(P + str(l) + "b1eff", [128, 2 * NJ], F32))
    b1ld = ph.enter_context(nc.sbuf_tensor(P + str(l) + "b1ld", [128, 2 * NJ], F32))
    lbS = ph.enter_context(nc.sbuf_tensor(P + str(l) + "lbS", [128, 8], BF16))
    b2row = ph.enter_context(nc.sbuf_tensor(P + str(l) + "b2row", [1, D], F32))
    ymt = [ph.enter_context(nc.sbuf_tensor(P + str(l) + f"ym{i}", [128, D], F32)) for i in range(2)]
    x1t2 = [ph.enter_context(nc.sbuf_tensor(P + str(l) + f"x1b{i}", [128, D], F32)) for i in range(2)]
    yot = [ph.enter_context(nc.sbuf_tensor(P + str(l) + f"yo{i}", [128, D], F32)) for i in range(2)]

    w2ld = ctx.psem(P + "w2ld")
    ztl = ctx.psem(P + "ztl")
    wsl = Ring(ctx, "swB", 2)
    tam = ctx.psem(P + "tam")
    tgm = ctx.psem(P + "tgm")
    bad = ctx.psem(P + "bad")
    bgd = ctx.psem(P + "bgd")
    gld = ctx.psem(P + "gld")
    gtd = ctx.psem(P + "gtd")
    wdm = ctx.psem(P + "wdm")
    wev = ctx.psem(P + "wev")
    yld2 = Ring(ctx, "hwB", 2)
    ost = Ring(ctx, "hwC", 2)
    ostg = Ring(ctx, "swC", 2)
    ostfin = ctx.sem("out_store")

    nslab = 2 * NJ

    @block.sync
    def _(sync):
        bld.inc(sync.dma_start(out=b1ld[:, 0:NJ],
                               in_=AP(ctx.dram[f"b1a{l}"].tensor, 0, [[1, 128], [128, NJ]]),
                               allow_slow_non_contiguous=True), 16)
        bld.inc(sync.dma_start(out=b1ld[:, NJ:2 * NJ],
                               in_=AP(ctx.dram[f"b1g{l}"].tensor, 0, [[1, 128], [128, NJ]]),
                               allow_slow_non_contiguous=True), 16)
        bld.inc(sync.dma_start(out=b2row[:],
                               in_=AP(ctx.dram[f"b2{l}"].tensor, 0, [[D, 1], [1, D]])), 16)
        for s in range(SUP):
            if s >= 1:
                tgm.wait(sync, s * NJ)
            zst.wait_all_issued(sync)
            for cc in range(8):
                ztl.inc(sync.dma_start_transpose(
                    z2T[:, cc * TSUP:(cc + 1) * TSUP],
                    z2_bf[s * TSUP:(s + 1) * TSUP, cc * 128:(cc + 1) * 128]), 16)
            for ti in range(TPS):
                gt = s * TPS + ti
                if gt >= 2:
                    wev.wait(sync, gt - 1)
                yld2.inc(sync.dma_start(out=ymt[gt % 2][:],
                                        in_=ymid_f[gt * 128:(gt + 1) * 128, :]), gt)
                if l == 1:
                    yld2.inc(sync.dma_start(out=x1t2[gt % 2][:],
                                            in_=ctx.dram["x1_f"][gt * 128:(gt + 1) * 128, :]), gt)
                wev.wait(sync, gt + 1)
                dst = ctx.dram["out"] if l == 1 else ctx.dram["yA_f"]
                inst = sync.dma_start(out=dst[gt * 128:(gt + 1) * 128, :], in_=yot[gt % 2][:])
                ost.inc(inst, gt)
                if l == 1:
                    ostfin.n += 0
        if l == 1:
            ost.wait_all_issued(sync)
            ostfin.inc(sync.nop())

    bldg = ctx.psem(P + "bldg")

    @block.gpsimd
    def _(g):
        bldg.inc(g.dma_start(out=lbS[:],
                             in_=AP(ctx.dram[f"lnb{l}"].tensor, 0, [[1, 128], [128, 8]]),
                             allow_slow_non_contiguous=True), 16)
        for jj in range(NJ):
            w2ld.inc(g.dma_start(out=w2sb[:, jj * D:(jj + 1) * D],
                                 in_=ctx.dram[f"w2{l}"][jj * 128:(jj + 1) * 128, :]), 16)
        def emit_bf_stores(sprev):
            if l != 0:
                return
            for ti in range(TPS):
                gt = sprev * TPS + ti
                wev.wait(g, gt + 1)
                ostg.inc(g.dma_start(out=ctx.dram["yA_bf"][gt * 128:(gt + 1) * 128, :],
                                     in_=yot[gt % 2][:]), gt)

        for s in range(SUP):
            if s >= 1:
                emit_bf_stores(s - 1)
            for j in range(NJ):
                for ab in range(2):
                    k = s * nslab + 2 * j + ab
                    if k >= 2:
                        (tam if ab == 0 else tgm).wait(g, s * NJ + j)
                    src = ctx.dram[f"w1a{l}"] if ab == 0 else ctx.dram[f"w1g{l}"]
                    wsl.inc(g.dma_start(
                        out=w1sl[k % 2][:],
                        in_=AP(src.tensor, j * 128, [[HIDP, 128], [128 * HIDP, 8], [1, 128]])), k)
        emit_bf_stores(SUP - 1)

    @block.tensor
    def _(t):
        w2ld.wait(t)
        bld.wait(t)
        bldg.wait(t)
        for s in range(SUP):
            ztl.wait(t, 16 * 8 * (s + 1))
            for j in range(NJ):
                sj = s * NJ + j
                for ab in range(2):
                    k = s * nslab + 2 * j + ab
                    wsl.wait(t, k)
                    pp = pa if ab == 0 else pg
                    if sj >= 1:
                        gtd.wait(t, sj)
                        gld.wait(t, sj)
                    for cc in range(8):
                        lhsT = w1sl[k % 2][:, cc * 128:(cc + 1) * 128]
                        for tg in range(2):
                            instr = t.matmul(pp[:, tg * 512:(tg + 1) * 512], lhsT,
                                             z2T[:, cc * TSUP + tg * 512: cc * TSUP + (tg + 1) * 512],
                                             start=(cc == 0), stop=(cc == 7))
                        if s == 0:
                            instr = t.matmul(pp[:, 1024:1025], lhsT, lbS[:, cc:cc + 1],
                                             start=(cc == 0), stop=(cc == 7))
                    if ab == 0:
                        tam.inc(instr)
                    else:
                        tgm.inc(instr)
            for ti in range(TPS):
                gt = s * TPS + ti
                if gt >= 1:
                    wev.wait(t, gt)
                for g2 in range(2):
                    t.matmul(pd[:, g2 * 512:(g2 + 1) * 512], sb["ones1"][:],
                             b2row[:, g2 * 512:(g2 + 1) * 512], start=True, stop=False)
                gtd.wait(t, (s + 1) * NJ)
                for j in range(NJ):
                    lhsT = gatedT[:, j * TSUP + ti * 128: j * TSUP + (ti + 1) * 128]
                    for g2 in range(2):
                        instr = t.matmul(pd[:, g2 * 512:(g2 + 1) * 512], lhsT,
                                         w2sb[:, j * D + g2 * 512: j * D + (g2 + 1) * 512],
                                         start=False, stop=(j == NJ - 1))
                wdm.inc(instr)

    @block.scalar
    def _(s_e):
        for s in range(SUP):
            for j in range(NJ):
                sj = s * NJ + j
                tgm.wait(s_e, sj + 1)
                if sj >= 1:
                    gtd.wait(s_e, sj)
                if s == 0:
                    bgd.wait(s_e, j + 1)
                gld.inc(s_e.activation(geluSb[:], pg[:, 0:TSUP], AF.Gelu,
                                       bias=b1eff[:, NJ + j:NJ + j + 1]))

    @block.vector
    def _(v):
        lrepe.wait(v, 1)
        bld.wait(v, 48)
        for s in range(SUP):
            for j in range(NJ):
                sj = s * NJ + j
                if s == 0:
                    tam.wait(v, sj + 1)
                    bad.inc(v.tensor_add(b1eff[:, j:j + 1], pa[:, 1024:1025], b1ld[:, j:j + 1]))
                    tgm.wait(v, sj + 1)
                    bgd.inc(v.tensor_add(b1eff[:, NJ + j:NJ + j + 1], pg[:, 1024:1025],
                                         b1ld[:, NJ + j:NJ + j + 1]))
                    v.drain()
                if s >= 1:
                    wdm.wait(v, s * TPS)
                gld.wait(v, sj + 1)
                tam.wait(v, sj + 1)
                gtd.inc(v.scalar_tensor_tensor(
                    gatedT[:, j * TSUP:(j + 1) * TSUP], pa[:, 0:TSUP],
                    b1eff[:, j:j + 1], geluSb[:], op0=ALU.add, op1=ALU.mult))
            for ti in range(TPS):
                gt = s * TPS + ti
                wdm.wait(v, gt + 1)
                yld2.wait(v, gt, k=(32 if l == 1 else 16))
                if gt >= 2:
                    ost.wait(v, gt - 2)
                    if l == 0:
                        ostg.wait(v, gt - 2)
                inst = v.tensor_add(yot[gt % 2][:], pd[:], ymt[gt % 2][:])
                if l == 1:
                    v.drain()
                    inst = v.tensor_add(yot[gt % 2][:], yot[gt % 2][:], x1t2[gt % 2][:])
                wev.inc(inst)



def _prep_inputs(inputs):
    f32 = np.float32
    x = np.ascontiguousarray(inputs["L0_attn_x"])
    w1 = inputs["L0_ffn_w1"]
    base = {
        "wqkvA": np.ascontiguousarray(inputs["L0_attn_w_qkv"].T),
        "gammaA": np.ascontiguousarray(inputs["L0_attn_gamma"]),
        "woutT": np.ascontiguousarray(inputs["L0_attn_w_out"].T),
    }
    for l in range(2):
        base[f"wqkvF{l}"] = np.ascontiguousarray(inputs["L0_ffn_wqkv"][l])
        base[f"qs{l}"] = np.ascontiguousarray(inputs["L0_ffn_q_scale"][l])
        base[f"ks{l}"] = np.ascontiguousarray(inputs["L0_ffn_k_scale"][l])
        base[f"wo{l}"] = np.ascontiguousarray(inputs["L0_ffn_wo"][l])
        base[f"lng{l}"] = np.ascontiguousarray(inputs["L0_ffn_ln_g"][l])
        base[f"lnb{l}"] = np.ascontiguousarray(inputs["L0_ffn_ln_b"][l])
        w1a = np.zeros((D, HIDP), f32)
        w1g = np.zeros((D, HIDP), f32)
        w1a[:, :HID] = w1[l][:, :HID]
        w1g[:, :HID] = w1[l][:, HID:]
        base[f"w1a{l}"] = w1a
        base[f"w1g{l}"] = w1g
        b1a = np.zeros((HIDP,), f32)
        b1g = np.zeros((HIDP,), f32)
        b1a[:HID] = inputs["L0_ffn_b1"][l][:HID]
        b1g[:HID] = inputs["L0_ffn_b1"][l][HID:]
        base[f"b1a{l}"] = b1a
        base[f"b1g{l}"] = b1g
        w2p = np.zeros((HIDP, D), f32)
        w2p[:HID] = inputs["L0_ffn_w2"][l]
        base[f"w2{l}"] = w2p
        base[f"b2{l}"] = np.ascontiguousarray(inputs["L0_ffn_b2"][l])
    in_maps = []
    for c in range(NCORE):
        m = dict(base)
        m["x"] = np.ascontiguousarray(x[c])
        in_maps.append(m)
    return in_maps


def kernel(**inputs):
    key = (tuple(_TAPS), _PHASES)
    if key not in _BUILD_CACHE:
        _BUILD_CACHE[key] = build_nc(taps=_TAPS, phases=_PHASES)
    nc = _BUILD_CACHE[key]
    in_maps = _prep_inputs(inputs)
    res = run_bass_kernel_spmd(nc, in_maps, core_ids=list(range(NCORE)))
    out = np.stack([np.asarray(r["out"]) for r in res.results], axis=0)
    if _TAPS:
        kernel.last_taps = [{k: np.asarray(v) for k, v in r.items()} for r in res.results]
    return out


# version 10
# speedup vs baseline: 2.1690x; 1.0091x over previous
import math
from contextlib import ExitStack

import numpy as np
import ml_dtypes
import os as _os
import concourse.bass as bass
import concourse.mybir as mybir
from concourse.bass_utils import run_bass_kernel_spmd

F32 = mybir.dt.float32
BF16 = mybir.dt.bfloat16
I32 = mybir.dt.int32
ALU = mybir.AluOpType
AF = mybir.ActivationFunctionType
AX = mybir.AxisListType.X
AP = bass.AP

NCORE = 8
NTOK = 4096
NT = 32
D = 1024
HEADS_A, DH_A = 8, 128
HEADS_F, DH_F = 8, 64
WIN = 512
HID = 2730
HIDP = 2816
NJ = HIDP // 128

def _trunc12(x):
    u = np.array(np.float32(x), np.float32).view(np.uint32) & np.uint32(0xFFFFF000)
    return float(np.array(u, np.uint32).view(np.float32))

_PI = float(np.pi)
P1C = _trunc12(_PI)
P2C = _trunc12(_PI - P1C)
P3C = float(np.float32(_PI - P1C - P2C))
H1C = float(np.float32(_PI / 2))
H2C = float(np.float32(_PI / 2 - H1C))
INV_PI = float(np.float32(1.0 / _PI))
L1C = _trunc12(math.log(10000.0))
L2C = _trunc12(math.log(10000.0) - L1C)
L3C = float(np.float32(math.log(10000.0) - L1C - L2C))
C_HALFPI = 1.5707964
NEG_BIG = -1e9
SCALE_ATTN = 128.0 ** -0.5

_TAPS = ()
_PHASES = 99

_BUILD_CACHE = {}


class Sem:
    __slots__ = ("h", "n", "base")

    def __init__(self, nc, es, name):
        self.h = es.enter_context(nc.semaphore(name))
        self.n = 0
        self.base = 0

    def rebase(self):
        self.base = self.n
        return self

    def inc(self, instr, k=1):
        instr.then_inc(self.h, k)
        self.n += k
        return instr

    def wait(self, eng, n=None):
        eng.wait_ge(self.h, self.n if n is None else self.base + n)


class Ring:

    def __init__(self, ctx, name, slots):
        self.sems = [ctx.psem(f"{name}{i}") for i in range(slots)]
        self.slots = slots

    def inc(self, instr, i, k=16):
        self.sems[i % self.slots].inc(instr, k)

    def wait(self, eng, i, k=16):
        s = self.sems[i % self.slots]
        eng.wait_ge(s.h, s.base + k * (i // self.slots + 1))

    def wait_all_issued(self, eng):
        for s in self.sems:
            s.wait(eng)


class Ctx:
    def __init__(self, nc, es, block):
        self.nc = nc
        self.es = es
        self.block = block
        self.sems = {}
        self.dram = {}
        self.sb = {}

    def sem(self, name):
        if name not in self.sems:
            self.sems[name] = Sem(self.nc, self.es, name)
        return self.sems[name]

    def psem(self, name):
        return self.sem(name).rebase()

    def barrier(self, name):
        s = self.sem("bar")
        b = self.block
        cur = [(ss.h, ss.n) for nm, ss in self.sems.items() if nm != "bar"]
        for sec in (b.sync, b.gpsimd, b.vector, b.scalar, b.tensor):
            @sec
            def _(eng, cur=cur):
                for h, n in cur:
                    if n:
                        eng.wait_ge(h, n)

        @b.vector
        def _(v, s=s):
            s.inc(v.memset(self.sb["barv"][:1, 0:1], 0.0))

        @b.scalar
        def _(sc, s=s):
            s.inc(sc.activation(self.sb["barv"][:1, 1:2], self.sb["barv"][:1, 1:2],
                                AF.Copy, scale=0.0))

        @b.gpsimd
        def _(g, s=s):
            s.inc(g.memset(self.sb["barv"][:1, 2:3], 0.0))

        @b.sync
        def _(eng, s=s):
            s.inc(eng.nop())

        @b.tensor
        def _(eng, s=s):
            s.inc(eng.nop())
        tgt = s.n
        for sec in (b.sync, b.gpsimd, b.vector, b.scalar, b.tensor):
            @sec
            def _(eng, s=s, tgt=tgt):
                eng.wait_ge(s.h, tgt)



def build_nc(taps=(), phases=99):
    nc = bass.Bass("TRN2", target_bir_lowering=False, detect_race_conditions=False)
    es = ExitStack()

    def dram(name, shape, dt, kind=None):
        k = kind
        if k is None:
            k = "ExternalOutput" if name in taps else "Internal"
        if k == "Internal":
            t = nc.dram_tensor(name, shape, dt)
        else:
            t = nc.dram_tensor(name, shape, dt, kind=k)
        return t.ap()

    io = {}
    io["x"] = dram("x", [NTOK, D], F32, "ExternalInput")
    io["wqkvA"] = dram("wqkvA", [D, 3 * D], F32, "ExternalInput")
    io["gammaA"] = dram("gammaA", [D], F32, "ExternalInput")
    io["woutT"] = dram("woutT", [D, D], F32, "ExternalInput")
    for l in range(2):
        io[f"wqkvF{l}"] = dram(f"wqkvF{l}", [D, 1536], F32, "ExternalInput")
        io[f"qs{l}"] = dram(f"qs{l}", [64], F32, "ExternalInput")
        io[f"ks{l}"] = dram(f"ks{l}", [64], F32, "ExternalInput")
        io[f"wo{l}"] = dram(f"wo{l}", [512, D], F32, "ExternalInput")
        io[f"lng{l}"] = dram(f"lng{l}", [D], F32, "ExternalInput")
        io[f"lnb{l}"] = dram(f"lnb{l}", [D], F32, "ExternalInput")
        io[f"w1a{l}"] = dram(f"w1a{l}", [D, HIDP], F32, "ExternalInput")
        io[f"w1g{l}"] = dram(f"w1g{l}", [D, HIDP], F32, "ExternalInput")
        io[f"b1a{l}"] = dram(f"b1a{l}", [HIDP], F32, "ExternalInput")
        io[f"b1g{l}"] = dram(f"b1g{l}", [HIDP], F32, "ExternalInput")
        io[f"w2{l}"] = dram(f"w2{l}", [HIDP, D], F32, "ExternalInput")
        io[f"b2{l}"] = dram(f"b2{l}", [D], F32, "ExternalInput")
    io["out"] = dram("out", [NTOK, D], F32, "ExternalOutput")

    io["z_bf"] = dram("z_bf", [NTOK, D], BF16)
    io["qkvA_bf"] = dram("qkvA_bf", [NTOK, 3 * D], BF16)
    io["x1_f"] = dram("x1_f", [NTOK, D], F32)
    io["yA_f"] = dram("yA_f", [NTOK, D], F32)
    io["yB_f"] = dram("yB_f", [NTOK, D], F32)
    io["yA_bf"] = dram("yA_bf", [NTOK, D], BF16)
    io["z2_bf"] = dram("z2_bf", [NTOK, D], BF16)
    io["qkvF_bf"] = dram("qkvF_bf", [NTOK, 1536], BF16)
    io["qr_bf"] = dram("qr_bf", [NTOK, 512], BF16)
    io["kr_bf"] = dram("kr_bf", [NTOK, 512], BF16)

    with es:
        block = es.enter_context(nc.Block())
        ctx = Ctx(nc, es, block)
        ctx.dram = io

        sb = ctx.sb
        sb["mdiag"] = es.enter_context(nc.sbuf_tensor("mdiag", [128, 128], F32))
        sb["mfar"] = es.enter_context(nc.sbuf_tensor("mfar", [128, 128], F32))
        sb["ident"] = es.enter_context(nc.sbuf_tensor("ident", [128, 128], BF16))
        sb["ones1"] = es.enter_context(nc.sbuf_tensor("ones1", [1, 128], F32))
        sb["barv"] = es.enter_context(nc.sbuf_tensor("barv", [1, 4], F32))
        CONSTS = [1e-5, 1e-30]
        sb["cpool"] = es.enter_context(nc.sbuf_tensor("cpool", [128, len(CONSTS)], F32))
        for i, cv in enumerate(CONSTS):
            nc.const_aps.register(F32, cv, sb["cpool"][:, i:i + 1])
        ctx.cpool_consts = CONSTS

        def run_phase(idx, name, fn, *args):
            if phases > idx:
                with ExitStack() as ps:
                    ctx.phase_stack = ps
                    fn(ctx, *args)
            ctx.barrier(name)

        run_phase(0, "p0", emit_p0)
        run_phase(1, "p1a", emit_p1a_tables)
        run_phase(2, "p1b", emit_p1b)
        run_phase(3, "p1c", emit_p1c)
        run_phase(4, "p2pre", emit_p2pre)
        for l in range(2):
            run_phase(5 + 2 * l, f"p2a{l}", emit_p2a, l)
            run_phase(6 + 2 * l, f"p2b{l}", emit_p2b, l)
        st = ctx.sem("out_store")

        @block.sync
        def _(sync):
            sync.wait_ge(st.h, st.n)

    return nc



def emit_p1a_tables(ctx):
    if _os.environ.get("K_NO_TABLES") != "1":
        emit_tables(ctx)
    emit_p1a(ctx)


def emit_p0(ctx):
    nc, block = ctx.nc, ctx.block
    ph = ctx.phase_stack
    sb = ctx.sb
    g_done = ctx.sem("p0_g")
    idm = ph.enter_context(nc.sbuf_tensor("p0_idm", [128, 128], F32))

    @block.gpsimd
    def _(g):
        g_done.inc(g.iota(idm[:], [[-1, 128]], channel_multiplier=1,
                          allow_small_or_imprecise_dtypes=True))
        for i, cv in enumerate(ctx.cpool_consts):
            g_done.inc(g.memset(ctx.sb["cpool"][:, i:i + 1], cv))

    @block.vector
    def _(v):
        g_done.wait(v)
        v.tensor_single_scalar(sb["ident"][:], idm[:], 0.0, op=ALU.is_equal)
        v.tensor_single_scalar(sb["mdiag"][:], idm[:], 0.0, op=ALU.is_lt)
        v.tensor_single_scalar(sb["mfar"][:], idm[:], 0.0, op=ALU.is_gt)
        v.drain()
        v.tensor_scalar_mul(sb["mdiag"][:], sb["mdiag"][:], NEG_BIG)
        v.tensor_scalar_mul(sb["mfar"][:], sb["mfar"][:], NEG_BIG)
        v.memset(sb["ones1"][:], 1.0)


def emit_tables(ctx):
    nc, block = ctx.nc, ctx.block
    ph = ctx.phase_stack
    es = ph
    sb = ctx.sb
    sb["Tcos"] = ph.enter_context(nc.sbuf_tensor("Tcos", [128, NT * 32], F32))
    sb["Tsin"] = ph.enter_context(nc.sbuf_tensor("Tsin", [128, NT * 32], F32))
    sb["Txs"] = ph.enter_context(nc.sbuf_tensor("Txs", [128, NT * 32], F32))
    sb["Trxs"] = ph.enter_context(nc.sbuf_tensor("Trxs", [128, NT * 32], F32))
    ctx.dram["tab_f"] = nc.dram_tensor("tab_f", [4 * 128, NT * 32], F32, kind="Internal").ap()
    g_done = ctx.sem("t0_g")
    v_done = ctx.sem("t0_v")
    s_done = ctx.sem("t0_s")

    t = {}
    for name, shape, dt in [
        ("jrow", [128, 32], F32), ("ptile", [128, 1], F32),
        ("u1", [128, 32], F32), ("u2", [128, 32], F32),
        ("ivf", [128, 32], F32), ("lnb", [128, 32], F32),
        ("ang", [128, 32], F32), ("angc", [128, 32], F32),
        ("yy", [128, 32], F32), ("mi", [128, 32], I32), ("mf", [128, 32], F32),
        ("rr", [128, 32], F32), ("a1", [128, 32], F32), ("a2", [128, 32], F32),
        ("hh", [128, 32], F32), ("hi", [128, 32], I32), ("sg", [128, 32], F32),
        ("sn", [128, 32], F32), ("pw", [128, 1], F32),
    ]:
        t[name] = es.enter_context(nc.sbuf_tensor("t0_" + name, shape, dt))

    @block.gpsimd
    def _(g):
        g_done.inc(g.iota(t["jrow"][:], [[1, 32]], channel_multiplier=0,
                          allow_small_or_imprecise_dtypes=True))

    @block.vector
    def _(v):
        g_done.wait(v)
        v.tensor_scalar(t["u1"][:], t["jrow"][:], L1C, None, op0=ALU.mult)
        v.tensor_scalar(t["u2"][:], t["jrow"][:], L2C, None, op0=ALU.mult)
        v.drain()
        v.tensor_add(t["u1"][:], t["u1"][:], t["u2"][:])
        v.tensor_scalar(t["u2"][:], t["jrow"][:], L3C, None, op0=ALU.mult)
        v.drain()
        v_done.inc(v.tensor_add(t["u1"][:], t["u1"][:], t["u2"][:]))

    @block.scalar
    def _(s):
        v_done.wait(s)
        s_done.inc(s.activation(t["ivf"][:], t["u1"][:], AF.Exp, scale=-0.03125))

    @block.vector
    def _(v):
        s_done.wait(v)
        v.tensor_scalar(t["u2"][:], t["jrow"][:], 10.0, 128.0, op0=ALU.mult, op1=ALU.add)
        v.drain()
        v_done.inc(v.tensor_scalar(t["u2"][:], t["u2"][:], 1.0 / 448.0, None, op0=ALU.mult))

    @block.scalar
    def _(s):
        v_done.wait(s)
        s_done.inc(s.activation(t["lnb"][:], t["u2"][:], AF.Ln))

    def emit_sin(steps, x_name, out_ap):
        def v1(v):
            v.tensor_scalar(t["yy"][:], t[x_name][:], INV_PI, 0.5, op0=ALU.mult, op1=ALU.add)
        steps.append(("v", v1))
        def s1(s):
            s.copy(t["mi"][:], t["yy"][:])
            s.drain()
            s.copy(t["mf"][:], t["mi"][:])
        steps.append(("s", s1))
        def v2(v):
            v.scalar_tensor_tensor(t["rr"][:], t["mf"][:], -P1C, t[x_name][:], op0=ALU.mult, op1=ALU.add)
            v.drain()
            v.scalar_tensor_tensor(t["rr"][:], t["mf"][:], -P2C, t["rr"][:], op0=ALU.mult, op1=ALU.add)
            v.drain()
            v.scalar_tensor_tensor(t["rr"][:], t["mf"][:], -P3C, t["rr"][:], op0=ALU.mult, op1=ALU.add)
            v.drain()
            v.tensor_single_scalar(t["a1"][:], t["rr"][:], C_HALFPI, op=ALU.is_ge)
            v.tensor_single_scalar(t["a2"][:], t["rr"][:], -C_HALFPI, op=ALU.is_le)
            v.drain()
            v.tensor_sub(t["a1"][:], t["a1"][:], t["a2"][:])
            v.drain()
            v.tensor_add(t["mf"][:], t["mf"][:], t["a1"][:])
            v.scalar_tensor_tensor(t["rr"][:], t["a1"][:], -P1C, t["rr"][:], op0=ALU.mult, op1=ALU.add)
            v.drain()
            v.scalar_tensor_tensor(t["rr"][:], t["a1"][:], -P2C, t["rr"][:], op0=ALU.mult, op1=ALU.add)
            v.drain()
            v.scalar_tensor_tensor(t["rr"][:], t["a1"][:], -P3C, t["rr"][:], op0=ALU.mult, op1=ALU.add)
            v.drain()
            v.tensor_scalar(t["hh"][:], t["mf"][:], 0.5, None, op0=ALU.mult)
        steps.append(("v", v2))
        def s2(s):
            s.copy(t["hi"][:], t["hh"][:])
            s.drain()
            s.copy(t["sg"][:], t["hi"][:])
        steps.append(("s", s2))
        def v3(v):
            v.tensor_sub(t["sg"][:], t["hh"][:], t["sg"][:])
        steps.append(("v", v3))
        def s3(s):
            s.activation(t["sg"][:], t["sg"][:], AF.Abs)
            s.drain()
            s.activation(t["sn"][:], t["rr"][:], AF.Sin)
        steps.append(("s", s3))
        def v4(v, out_ap=out_ap):
            v.tensor_scalar(t["sg"][:], t["sg"][:], -4.0, 1.0, op0=ALU.mult, op1=ALU.add)
            v.drain()
            v.tensor_mul(out_ap, t["sn"][:], t["sg"][:])
        steps.append(("v", v4))

    for ti in range(NT):
        cols = slice(ti * 32, (ti + 1) * 32)
        steps = []

        @block.gpsimd
        def _(g, ti=ti):
            s_done.wait(g)
            v_done.wait(g)
            g_done.inc(g.iota(t["ptile"][:], [[1, 1]], base=ti * 128, channel_multiplier=1,
                              allow_small_or_imprecise_dtypes=True))

        def v0(v, ti=ti, cols=cols):
            g_done.wait(v)
            v.tensor_scalar(t["ang"][:], t["ivf"][:], t["ptile"][:, :1], None, op0=ALU.mult)
            v.drain()
            v.tensor_scalar(t["angc"][:], t["ang"][:], 1.0, H1C, op0=ALU.mult, op1=ALU.add)
            v.drain()
            v.tensor_scalar_add(t["angc"][:], t["angc"][:], H2C)
            v.tensor_scalar(t["pw"][:], t["ptile"][:], 1.0, -2048.0, op0=ALU.mult, op1=ALU.add)
            v.drain()
            v.tensor_scalar(t["pw"][:], t["pw"][:], 0.00390625, None, op0=ALU.mult)
            v.drain()
            v.tensor_scalar(t["u2"][:], t["lnb"][:], t["pw"][:, :1], None, op0=ALU.mult)
        steps.append(("v", v0))

        def s0(s, cols=cols):
            s.activation(sb["Txs"][:, cols], t["u2"][:], AF.Exp)
            s.activation(sb["Trxs"][:, cols], t["u2"][:], AF.Exp, scale=-1.0)
        steps.append(("s", s0))

        emit_sin(steps, "ang", sb["Tsin"][:, cols])
        emit_sin(steps, "angc", sb["Tcos"][:, cols])

        for engine, fn in steps:
            if engine == "v":
                @block.vector
                def _(v, fn=fn):
                    s_done.wait(v)
                    fn(v)
                    v_done.inc(v.memset(ctx.sb["barv"][:1, 3:4], 0.0))
            else:
                @block.scalar
                def _(s, fn=fn):
                    v_done.wait(s)
                    fn(s)
                    s_done.inc(s.activation(ctx.sb["barv"][:1, 1:2], ctx.sb["barv"][:1, 1:2],
                                            AF.Copy, scale=0.0))

    tstx = ctx.sem("t0_st")

    @block.sync
    def _(sync):
        s_done.wait(sync)
        v_done.wait(sync)
        for i, nm in enumerate(("Tcos", "Tsin", "Txs", "Trxs")):
            tstx.inc(sync.dma_start(out=ctx.dram["tab_f"][i * 128:(i + 1) * 128, :],
                                    in_=sb[nm][:]), 16)

    if _os.environ.get("K_DBG_TBL") == "1":
        dbg = ctx.sem("tdbg")

        @block.gpsimd
        def _(g):
            s_done.wait(g)
            v_done.wait(g)
            dbg.inc(g.dma_start(out=ctx.dram["qkvF_bf"][0:128, 0:1024], in_=sb["Tcos"][:]), 16)
            dbg.inc(g.dma_start(out=ctx.dram["qkvF_bf"][128:256, 0:1024], in_=sb["Tsin"][:]), 16)
            dbg.inc(g.dma_start(out=ctx.dram["qkvF_bf"][256:384, 0:1024], in_=sb["Txs"][:]), 16)
            dbg.inc(g.dma_start(out=ctx.dram["qkvF_bf"][384:512, 0:1024], in_=sb["Trxs"][:]), 16)
            g.wait_ge(dbg.h, dbg.n)




def emit_p1a(ctx):
    nc, es, block = ctx.nc, ctx.es, ctx.block
    x_in = ctx.dram["x"]
    z_bf = ctx.dram["z_bf"]

    ph = ctx.phase_stack
    xt = [ph.enter_context(nc.sbuf_tensor(f"a_x{i}", [128, D], F32)) for i in range(2)]
    ct = [ph.enter_context(nc.sbuf_tensor(f"a_c{i}", [128, D], F32)) for i in range(2)]
    zt = [ph.enter_context(nc.sbuf_tensor(f"a_z{i}", [128, D], F32)) for i in range(2)]
    dump = ph.enter_context(nc.sbuf_tensor("a_dump", [128, D], F32))
    st = [ph.enter_context(nc.sbuf_tensor(f"a_st{i}", [128, 8], F32)) for i in range(2)]
    ld = Ring(ctx, "hwA", 2)
    cfree = ctx.sem("a_cfree")
    ssd = ctx.sem("a_ssd")
    zs = ctx.sem("a_zs")
    stc = Ring(ctx, "swA", 2)

    @block.sync
    def _(sync):
        for ti in range(NT):
            if ti >= 2:
                cfree.wait(sync, ti - 1)
            ld.inc(sync.dma_start(out=xt[ti % 2][:], in_=x_in[ti * 128:(ti + 1) * 128, :]), ti)

    @block.vector
    def _(v):
        for ti in range(NT):
            x, c, s_ = xt[ti % 2], ct[ti % 2], st[ti % 2]
            ld.wait(v, ti)
            v.reduce_sum(s_[:, 0:1], x[:], axis=AX)
            v.drain()
            v.tensor_scalar(s_[:, 0:1], s_[:, 0:1], 1.0 / D, None, op0=ALU.mult)
            v.drain()
            cfree.inc(v.tensor_scalar(c[:], x[:], s_[:, 0:1], None, op0=ALU.subtract))
            v.drain()
            ssd.wait(v, ti + 1)
            v.tensor_mul(s_[:, 4:5], s_[:, 2:3], s_[:, 3:4])
            v.drain()
            v.tensor_scalar_max(s_[:, 4:5], s_[:, 4:5], 1e-12)
            v.drain()
            v.reciprocal(s_[:, 5:6], s_[:, 4:5])
            v.drain()
            v.tensor_mul(s_[:, 6:7], s_[:, 2:3], s_[:, 5:6])
            v.drain()
            v.tensor_scalar(s_[:, 6:7], s_[:, 6:7], 32.0, None, op0=ALU.mult)
            v.drain()
            if ti >= 2:
                stc.wait(v, ti - 2)
            zs.inc(v.tensor_scalar(zt[ti % 2][:], c[:], s_[:, 6:7], None, op0=ALU.mult))

    @block.scalar
    def _(s):
        for ti in range(NT):
            cfree.wait(s, ti + 1)
            if ti >= 2:
                zs.wait(s, ti - 1)
            c, s_ = ct[ti % 2], st[ti % 2]
            s.activation(dump[:], c[:], AF.Square, accum_out=s_[:, 1:2])
            s.drain()
            s.activation(s_[:, 2:3], s_[:, 1:2], AF.Ln, scale=1.0 / D, bias=1e-5)
            s.drain()
            s.activation(s_[:, 2:3], s_[:, 2:3], AF.Exp, scale=-0.5)
            s.drain()
            s.activation(s_[:, 3:4], s_[:, 1:2], AF.Ln, bias=1e-30)
            s.drain()
            ssd.inc(s.activation(s_[:, 3:4], s_[:, 3:4], AF.Exp, scale=0.5))

    @block.gpsimd
    def _(g):
        for ti in range(NT):
            zs.wait(g, ti + 1)
            stc.inc(g.dma_start(out=z_bf[ti * 128:(ti + 1) * 128, :], in_=zt[ti % 2][:]), ti)


def emit_p1b(ctx):
    nc, block = ctx.nc, ctx.block
    ph = ctx.phase_stack
    z_bf = ctx.dram["z_bf"]
    qkvA_bf = ctx.dram["qkvA_bf"]
    wqkvA = ctx.dram["wqkvA"]
    gammaA = ctx.dram["gammaA"]

    zT = ph.enter_context(nc.sbuf_tensor("b_zT", [128, 8 * NTOK], BF16))
    wq = ph.enter_context(nc.sbuf_tensor("b_wq", [128, 8 * 3072], BF16))
    gam = ph.enter_context(nc.sbuf_tensor("b_gam", [128, 8], F32))
    qv = [ph.enter_context(nc.sbuf_tensor(f"b_qv{i}", [128, 3072], F32)) for i in range(2)]
    qp = ph.enter_context(nc.psum_tensor("b_qp", [128, 3072], F32))

    ld_zt = ctx.sem("b_ldzt")
    ld_w = ctx.sem("b_ldw")
    ld_g = ctx.sem("b_ldg")
    wf = ctx.sem("b_wf")
    mmd = ctx.sem("b_mmd")
    ev = ctx.sem("b_ev")
    stv = Ring(ctx, "swA", 2)

    @block.sync
    def _(sync):
        ld_g.inc(sync.dma_start(out=gam[:], in_=AP(gammaA.tensor, 0, [[1, 128], [128, 8]]), allow_slow_non_contiguous=True), 16)
        for cc in range(8):
            ld_zt.inc(sync.dma_start_transpose(
                zT[:, cc * NTOK:(cc + 1) * NTOK],
                z_bf[:, cc * 128:(cc + 1) * 128]), 16)

    @block.gpsimd
    def _(g):
        for cc in range(8):
            ld_w.inc(g.dma_start(out=wq[:, cc * 3072:(cc + 1) * 3072],
                                 in_=wqkvA[cc * 128:(cc + 1) * 128, :]), 16)
        for ti in range(NT):
            ev.wait(g, 2 * (ti + 1))
            stv.inc(g.dma_start(out=qkvA_bf[ti * 128:(ti + 1) * 128, :], in_=qv[ti % 2][:]), ti)

    @block.vector
    def _(v):
        ld_g.wait(v)
        for cc in range(8):
            ld_w.wait(v, 16 * (cc + 1))
            sl = wq[:, cc * 3072:(cc + 1) * 3072]
            wf.inc(v.tensor_scalar(sl, sl, gam[:, cc:cc + 1], None, op0=ALU.mult))
        for ti in range(NT):
            mmd.wait(v, ti + 1)
            if ti >= 2:
                stv.wait(v, ti - 2)
            ev.inc(v.tensor_copy(qv[ti % 2][:, 0:1536], qp[:, 0:1536]))

    @block.scalar
    def _(s):
        for ti in range(NT):
            mmd.wait(s, ti + 1)
            if ti >= 2:
                stv.wait(s, ti - 2)
            ev.inc(s.copy(qv[ti % 2][:, 1536:3072], qp[:, 1536:3072]))

    @block.tensor
    def _(t):
        ld_zt.wait(t)
        wf.wait(t)
        for ti in range(NT):
            if ti >= 1:
                ev.wait(t, 2 * ti)
            for cc in range(8):
                lhsT = zT[:, cc * NTOK + ti * 128: cc * NTOK + (ti + 1) * 128]
                for g6 in range(6):
                    instr = t.matmul(qp[:, g6 * 512:(g6 + 1) * 512],
                                     lhsT,
                                     wq[:, cc * 3072 + g6 * 512: cc * 3072 + (g6 + 1) * 512],
                                     start=(cc == 0), stop=(cc == 7))
            mmd.inc(instr)


def emit_p1c(ctx):
    nc, block = ctx.nc, ctx.block
    ph = ctx.phase_stack
    sb = ctx.sb
    qkvA_bf = ctx.dram["qkvA_bf"]
    x_in = ctx.dram["x"]
    x1_f = ctx.dram["x1_f"]
    woutT = ctx.dram["woutT"]

    KT = ph.enter_context(nc.sbuf_tensor("c_KT", [128, 8 * NTOK], BF16))
    V = ph.enter_context(nc.sbuf_tensor("c_V", [128, NT * D], BF16))
    WO = ph.enter_context(nc.sbuf_tensor("c_WO", [128, 8 * D], BF16))
    QT = [ph.enter_context(nc.sbuf_tensor(f"c_QT{i}", [128, D], BF16)) for i in range(2)]
    SSb = ph.enter_context(nc.sbuf_tensor("c_SSb", [128, NTOK], F32))
    Pb = [ph.enter_context(nc.sbuf_tensor(f"c_Pb{i}", [128, NTOK], BF16)) for i in range(3)]
    PTsb = [ph.enter_context(nc.sbuf_tensor(f"c_PT{i}", [128, 512], BF16)) for i in range(2)]
    OTsb = ph.enter_context(nc.sbuf_tensor("c_OT", [128, D], BF16))
    sst = [ph.enter_context(nc.sbuf_tensor(f"c_sst{i}", [128, 16], F32)) for i in range(3)]
    lns = [ph.enter_context(nc.sbuf_tensor(f"c_lns{i}", [128, 8], F32)) for i in range(2)]
    xt = [ph.enter_context(nc.sbuf_tensor("c_x0", [128, D], F32))] * 2
    x1t = [ph.enter_context(nc.sbuf_tensor("c_x10", [128, D], F32))] * 2
    Sp = [ph.enter_context(nc.psum_tensor(f"c_Sp{i}", [128, 512], F32)) for i in range(2)]
    ptp = [ph.enter_context(nc.psum_tensor(f"c_ptp{i}", [128, 1024], BF16)) for i in range(2)]
    otp = [ph.enter_context(nc.psum_tensor(f"c_otp{i}", [128, 512], F32)) for i in range(2)]
    opp = ph.enter_context(nc.psum_tensor("c_opp", [128, 1024], F32))

    units = [(qt, h) for qt in range(NT) for h in range(8)]
    NU = len(units)

    def nch(qt):
        return (128 * (qt + 1) + 511) // 512

    def ngrp(qt):
        return (qt + 1 + 3) // 4

    cum_ch = [0]
    cum_gr = [0]
    for (qt, h) in units:
        cum_ch.append(cum_ch[-1] + nch(qt))
        cum_gr.append(cum_gr[-1] + ngrp(qt))

    kld = ctx.sem("c_kld")
    vld = ctx.sem("c_vld")
    wld = ctx.sem("c_wld")
    qld = Ring(ctx, "hwA", 2)
    xld = Ring(ctx, "hwB", 2)
    smm = ctx.sem("c_smm")
    sed = ctx.sem("c_sed")
    vmc = ctx.sem("c_vmc")
    vms = ctx.sem("c_vms")
    vmx = ctx.sem("c_vmx")
    expd = ctx.sem("c_expd")
    vrz = ctx.sem("c_vrz")
    pnd = ctx.sem("c_pnd")
    ttd = ctx.sem("c_ttd")
    evdv = ctx.sem("c_evdv")
    evds = ctx.sem("c_evds")
    pvd = ctx.sem("c_pvd")
    otd = ctx.sem("c_otd")
    opd = ctx.sem("c_opd")
    vxr = ctx.sem("c_vxr")
    stx = Ring(ctx, "hwC", 2)
    qsc = ctx.sem("c_qsc")
    vmu = ctx.sem("c_vmu")
    sq2 = ctx.sem("c_sq2")
    vy0 = ctx.sem("c_vy0")
    lnsf = Ring(ctx, "hwD", 2)
    lnsg = Ring(ctx, "swB", 2)
    yA_f = ctx.dram["yA_f"]
    yA_bf = ctx.dram["yA_bf"]

    @block.sync
    def _(sync):
        for hh in range(8):
            kld.inc(sync.dma_start_transpose(
                KT[:, hh * NTOK:(hh + 1) * NTOK],
                qkvA_bf[:, D + hh * 128: D + (hh + 1) * 128]), 16)
        vld.inc(sync.dma_start(
            out=AP(V, 0, [[NT * D, 128], [D, NT], [1, D]]),
            in_=AP(qkvA_bf.tensor, 2 * D, [[3 * D, 128], [128 * 3 * D, NT], [1, D]])), 16)
        for qt in range(NT):
            if qt >= 2:
                smm.wait(sync, cum_ch[(qt - 1) * 8])
            for hh in range(8):
                qld.inc(sync.dma_start_transpose(
                    QT[qt % 2][:, hh * 128:(hh + 1) * 128],
                    qkvA_bf[qt * 128:(qt + 1) * 128, hh * 128:(hh + 1) * 128]), qt)
            if qt >= 1:
                vy0.wait(sync, qt)
                lnsf.inc(sync.dma_start(out=yA_f[(qt - 1) * 128:qt * 128, :],
                                        in_=xt[(qt - 1) % 2][:]), qt - 1)
                lnsf.wait(sync, qt - 1)
                lnsg.wait(sync, qt - 1)
                vxr.wait(sync, qt)
            xld.inc(sync.dma_start(out=xt[qt % 2][:], in_=x_in[qt * 128:(qt + 1) * 128, :]), qt)
            if qt >= 1:
                vxr.wait(sync, qt)
                stx.inc(sync.dma_start(out=x1_f[(qt - 1) * 128:qt * 128, :],
                                       in_=x1t[(qt - 1) % 2][:]), qt - 1)
        vxr.wait(sync, NT)
        stx.inc(sync.dma_start(out=x1_f[(NT - 1) * 128:NT * 128, :], in_=x1t[(NT - 1) % 2][:]), NT - 1)
        vy0.wait(sync, NT)
        lnsf.inc(sync.dma_start(out=yA_f[(NT - 1) * 128:NT * 128, :], in_=xt[(NT - 1) % 2][:]), NT - 1)

    @block.gpsimd
    def _(g):
        for hh in range(8):
            wld.inc(g.dma_start(out=WO[:, hh * D:(hh + 1) * D],
                                in_=woutT[hh * 128:(hh + 1) * 128, :]), 16)
        for qt in range(NT):
            vy0.wait(g, qt + 1)
            lnsg.inc(g.dma_start(out=yA_bf[qt * 128:(qt + 1) * 128, :], in_=xt[qt % 2][:]), qt)

    @block.tensor
    def _(t):
        kld.wait(t)
        vld.wait(t)
        wld.wait(t)

        def t_smm(u):
            qt, h = units[u]
            klen = 128 * (qt + 1)
            if h == 0:
                qld.wait(t, qt, k=128)
                qsc.wait(t, qt + 1)
            for c in range(nch(qt)):
                w = min(512, klen - c * 512)
                gidx = cum_ch[u] + c
                if gidx >= 2:
                    sed.wait(t, gidx - 1)
                instr = t.matmul(Sp[gidx % 2][:, :w],
                                 QT[qt % 2][:, h * 128:(h + 1) * 128],
                                 KT[:, h * NTOK + c * 512: h * NTOK + c * 512 + w],
                                 start=True, stop=True)
                smm.inc(instr)

        def t_back(u):
            qt, h = units[u]
            ng_ = ngrp(qt)
            pnd.wait(t, u + 1)
            if u >= 2:
                otd.wait(t, u - 1)

            def emit_T(grp):
                c0 = grp * 4
                cn = min(4, (qt + 1) - c0)
                gg = cum_gr[u] + grp
                if gg >= 2:
                    (evdv if gg % 2 == 0 else evds).wait(t, gg // 2)
                for ci in range(cn):
                    c = c0 + ci
                    instr = t.transpose(ptp[gg % 2][:, ci * 128:(ci + 1) * 128],
                                        Pb[u % 3][:, c * 128:(c + 1) * 128], sb["ident"][:])
                ttd.inc(instr)

            def emit_PV(grp):
                c0 = grp * 4
                cn = min(4, (qt + 1) - c0)
                gg = cum_gr[u] + grp
                (evdv if gg % 2 == 0 else evds).wait(t, gg // 2 + 1)
                for ci in range(cn):
                    c = c0 + ci
                    instr = t.matmul(otp[u % 2][:, 0:128],
                                     V[:, c * D + h * 128: c * D + (h + 1) * 128],
                                     PTsb[gg % 2][:, ci * 128:(ci + 1) * 128],
                                     start=(c == 0), stop=(c == qt))
                pvd.inc(instr)

            emit_T(0)
            for grp in range(1, ng_):
                emit_T(grp)
                emit_PV(grp - 1)
            emit_PV(ng_ - 1)
            if h == 7:
                otd.wait(t, u + 1)
                if qt >= 1:
                    vxr.wait(t, qt)
                for hh in range(8):
                    for g2 in range(2):
                        instr = t.matmul(opp[:, g2 * 512:(g2 + 1) * 512],
                                         OTsb[:, hh * 128:(hh + 1) * 128],
                                         WO[:, hh * D + g2 * 512: hh * D + (g2 + 1) * 512],
                                         start=(hh == 0), stop=(hh == 7))
                opd.inc(instr)

        for u in range(NU + 2):
            if u < NU:
                t_smm(u)
            if u >= 2:
                t_back(u - 2)

    @block.scalar
    def _(s):
        def s_evac(u):
            qt, h = units[u]
            klen = 128 * (qt + 1)
            if u >= 1:
                vms.wait(s, u)
                vmc.wait(s, cum_ch[u])
            for c in range(nch(qt)):
                w = min(512, klen - c * 512)
                gidx = cum_ch[u] + c
                smm.wait(s, gidx + 1)
                sed.inc(s.copy(SSb[:, c * 512:c * 512 + w], Sp[gidx % 2][:, :w]))

        def s_exp(u):
            qt, h = units[u]
            klen = 128 * (qt + 1)
            vmx.wait(s, u + 1)
            vms.wait(s, u + 1)
            if u >= 3:
                ttd.wait(s, cum_gr[u - 2])
                pnd.wait(s, u - 2)
                vrz.wait(s, u - 2)
            expd.inc(s.activation(Pb[u % 3][:, :klen], SSb[:, :klen], AF.Exp,
                                  bias=sst[u % 3][:, 0:1], accum_out=sst[u % 3][:, 1:2]))

        def s_tail(u):
            qt, h = units[u]
            ng_ = ngrp(qt)
            for grp in range(ng_):
                gg = cum_gr[u] + grp
                if gg % 2 != 1:
                    continue
                cn = min(4, (qt + 1) - grp * 4)
                ttd.wait(s, gg + 1)
                if gg >= 2:
                    pvd.wait(s, gg - 1)
                evds.inc(s.copy(PTsb[1][:, :cn * 128], ptp[1][:, :cn * 128]))

        def s_ln(qt):
            l_ = lns[qt % 2]
            vxr.wait(s, qt + 1)
            vmu.wait(s, qt + 1)
            s.activation(xt[qt % 2][:], x1t[qt % 2][:], AF.Square, accum_out=l_[:, 2:3])
            s.drain()
            s.activation(l_[:, 4:5], l_[:, 2:3], AF.Ln, scale=1.0 / D, bias=l_[:, 3:4])
            s.drain()
            sq2.inc(s.activation(l_[:, 4:5], l_[:, 4:5], AF.Exp, scale=-0.5))

        for u in range(NU + 2):
            if u >= 1 and u - 1 < NU:
                s_exp(u - 1)
            if u < NU:
                s_evac(u)
            if u >= 10 and (u - 10) % 8 == 0:
                s_ln((u - 10) // 8)
            if u >= 2:
                s_tail(u - 2)
        s_ln(NT - 1)

    @block.vector
    def _(v):
        def v_s1(u):
            qt, h = units[u]
            klen = 128 * (qt + 1)
            nc_ = nch(qt)
            sp_ = sst[u % 3]
            if h == 0:
                qld.wait(v, qt, k=128)
                qsc.inc(v.tensor_scalar(QT[qt % 2][:], QT[qt % 2][:], SCALE_ATTN, None,
                                        op0=ALU.mult))
            for c in range(nc_):
                w = min(512, klen - c * 512)
                gidx = cum_ch[u] + c
                sed.wait(v, gidx + 1)
                vmc.inc(v.reduce_max(sp_[:, 8 + c:9 + c], SSb[:, c * 512:c * 512 + w], axis=AX))
            v.drain()
            v.reduce_max(sp_[:, 3:4], sp_[:, 8:8 + nc_], axis=AX)
            v.drain()
            if u >= 3:
                expd.wait(v, u - 2)
            vmx.inc(v.tensor_scalar(sp_[:, 0:1], sp_[:, 3:4], -1.0, None, op0=ALU.mult))
            if u >= 1:
                expd.wait(v, u)
            vms.inc(v.tensor_add(SSb[:, klen - 128:klen], SSb[:, klen - 128:klen],
                                 sb["mdiag"][:]))

        def v_s2(u):
            qt, h = units[u]
            klen = 128 * (qt + 1)
            sp_ = sst[u % 3]
            expd.wait(v, u + 1)
            if u >= 3:
                pnd.wait(v, u - 2)
            vrz.inc(v.reciprocal(sp_[:, 2:3], sp_[:, 1:2]))
            v.drain()
            pnd.inc(v.tensor_scalar(Pb[u % 3][:, :klen], Pb[u % 3][:, :klen],
                                    sp_[:, 2:3], None, op0=ALU.mult))

        def v_s3(u):
            qt, h = units[u]
            ng_ = ngrp(qt)
            for grp in range(ng_):
                gg = cum_gr[u] + grp
                if gg % 2 != 0:
                    continue
                cn = min(4, (qt + 1) - grp * 4)
                ttd.wait(v, gg + 1)
                if gg >= 2:
                    pvd.wait(v, gg - 1)
                evdv.inc(v.tensor_copy(PTsb[0][:, :cn * 128], ptp[0][:, :cn * 128]))
            pvd.wait(v, cum_gr[u + 1])
            if qt >= 1:
                opd.wait(v, qt)
            otd.inc(v.tensor_copy(OTsb[:, h * 128:(h + 1) * 128], otp[u % 2][:, 0:128]))
            if h == 7:
                opd.wait(v, qt + 1)
                xld.wait(v, qt)
                if qt >= 1:
                    stx.wait(v, qt - 1)
                    sq2.wait(v, qt)
                vxr.inc(v.tensor_add(x1t[qt % 2][:], opp[:], xt[qt % 2][:]))
                l_ = lns[qt % 2]
                v.reduce_sum(l_[:, 0:1], x1t[qt % 2][:], axis=AX)
                v.drain()
                v.tensor_scalar(l_[:, 1:2], l_[:, 0:1], 1.0 / D, None, op0=ALU.mult)
                v.drain()
                v.tensor_mul(l_[:, 3:4], l_[:, 1:2], l_[:, 1:2])
                v.drain()
                vmu.inc(v.tensor_scalar(l_[:, 3:4], l_[:, 3:4], -1.0, 1e-5,
                                        op0=ALU.mult, op1=ALU.add))
            if h == 1 and qt >= 1:
                l_ = lns[(qt - 1) % 2]
                sq2.wait(v, qt)
                vy0.inc(v.tensor_scalar(xt[(qt - 1) % 2][:], x1t[(qt - 1) % 2][:],
                                        l_[:, 1:2], l_[:, 4:5],
                                        op0=ALU.subtract, op1=ALU.mult))

        for u in range(NU + 2):
            if u < NU:
                v_s1(u)
            if 1 <= u and u - 1 < NU:
                v_s2(u - 1)
            if u >= 2:
                v_s3(u - 2)
        l_ = lns[(NT - 1) % 2]
        sq2.wait(v, NT)
        vy0.inc(v.tensor_scalar(xt[(NT - 1) % 2][:], x1t[(NT - 1) % 2][:],
                                l_[:, 1:2], l_[:, 4:5],
                                op0=ALU.subtract, op1=ALU.mult))

def emit_p2pre(ctx):
    pass



def emit_ln_stream(ctx, tag, src_f, dst_f, dst_bf):
    nc, block = ctx.nc, ctx.block
    ph = ctx.phase_stack
    xt = [ph.enter_context(nc.sbuf_tensor(f"{tag}_x{i}", [128, D], F32)) for i in range(2)]
    yt = [ph.enter_context(nc.sbuf_tensor(f"{tag}_y{i}", [128, D], F32)) for i in range(2)]
    dump = ph.enter_context(nc.sbuf_tensor(f"{tag}_dump", [128, D], F32))
    st = [ph.enter_context(nc.sbuf_tensor(f"{tag}_st{i}", [128, 4], F32)) for i in range(2)]
    ld = Ring(ctx, "hwA", 2)
    cfree = ctx.sem(tag + "_cfree")
    ssd = ctx.sem(tag + "_ssd")
    yd = ctx.sem(tag + "_yd")
    stc = Ring(ctx, "hwB", 2)
    stcg = Ring(ctx, "swA", 2)

    @block.sync
    def _(sync):
        for ti in range(NT):
            if ti >= 2:
                yd.wait(sync, ti - 1)
            ld.inc(sync.dma_start(out=xt[ti % 2][:], in_=src_f[ti * 128:(ti + 1) * 128, :]), ti)
            if ti >= 1:
                yd.wait(sync, ti)
                stc.inc(sync.dma_start(out=dst_f[(ti - 1) * 128:ti * 128, :],
                                       in_=yt[(ti - 1) % 2][:]), ti - 1)
        yd.wait(sync, NT)
        stc.inc(sync.dma_start(out=dst_f[(NT - 1) * 128:NT * 128, :], in_=yt[(NT - 1) % 2][:]), NT - 1)

    @block.vector
    def _(v):
        for ti in range(NT):
            x, s_ = xt[ti % 2], st[ti % 2]
            ld.wait(v, ti)
            v.reduce_sum(s_[:, 0:1], x[:], axis=AX)
            v.drain()
            v.tensor_scalar(s_[:, 0:1], s_[:, 0:1], 1.0 / D, None, op0=ALU.mult)
            v.drain()
            cfree.inc(v.tensor_scalar(x[:], x[:], s_[:, 0:1], None, op0=ALU.subtract))
            v.drain()
            ssd.wait(v, ti + 1)
            if ti >= 2:
                stc.wait(v, ti - 2)
                stcg.wait(v, ti - 2)
            yd.inc(v.tensor_scalar(yt[ti % 2][:], x[:], s_[:, 2:3], None, op0=ALU.mult))

    @block.scalar
    def _(s):
        for ti in range(NT):
            cfree.wait(s, ti + 1)
            if ti >= 2:
                yd.wait(s, ti - 1)
            c, s_ = xt[ti % 2], st[ti % 2]
            s.activation(dump[:], c[:], AF.Square, accum_out=s_[:, 1:2])
            s.drain()
            s.activation(s_[:, 2:3], s_[:, 1:2], AF.Ln, scale=1.0 / D, bias=1e-5)
            s.drain()
            ssd.inc(s.activation(s_[:, 2:3], s_[:, 2:3], AF.Exp, scale=-0.5))

    @block.gpsimd
    def _(g):
        for ti in range(NT):
            yd.wait(g, ti + 1)
            stcg.inc(g.dma_start(out=dst_bf[ti * 128:(ti + 1) * 128, :], in_=yt[ti % 2][:]), ti)


def emit_p2a(ctx, l):
    nc, block, sb = ctx.nc, ctx.block, ctx.sb
    ph = ctx.phase_stack
    yin_f = ctx.dram["yA_f"]
    yin_bf = ctx.dram["yA_bf"]
    ymid_f = ctx.dram["yB_f"]
    qr_bf = ctx.dram["qr_bf"]
    kr_bf = ctx.dram["kr_bf"]
    P = "g"

    V = ph.enter_context(nc.sbuf_tensor(P + str(l) + "V", [128, NT * 512], BF16))
    for nm in ("Tcos", "Tsin", "Txs", "Trxs"):
        sb[nm] = ph.enter_context(nc.sbuf_tensor(P + str(l) + nm, [128, NT * 32], F32))
    tabl = ctx.psem(P + "tabl")

    @block.sync
    def _(sync):
        for i, nm in enumerate(("Tcos", "Tsin", "Txs", "Trxs")):
            tabl.inc(sync.dma_start(out=sb[nm][:],
                                    in_=ctx.dram["tab_f"][i * 128:(i + 1) * 128, :]), 16)
    s1 = ExitStack()
    syT = s1.enter_context(nc.sbuf_tensor(P + str(l) + "syT", [128, 8 * NTOK], BF16))
    wqf = s1.enter_context(nc.sbuf_tensor(P + str(l) + "wqf", [128, 8 * 1536], BF16))
    tmpT = s1.enter_context(nc.sbuf_tensor(P + str(l) + "tmpT", [128, NTOK], BF16))
    qsrow = s1.enter_context(nc.sbuf_tensor(P + str(l) + "qsrow", [1, 64], F32))
    ksrow = s1.enter_context(nc.sbuf_tensor(P + str(l) + "ksrow", [1, 64], F32))
    qsrep = s1.enter_context(nc.sbuf_tensor(P + str(l) + "qsrep", [128, 64], F32))
    ksrep = s1.enter_context(nc.sbuf_tensor(P + str(l) + "ksrep", [128, 64], F32))
    qsB = s1.enter_context(nc.sbuf_tensor(P + str(l) + "qsB", [128, 64], F32))
    ksB = s1.enter_context(nc.sbuf_tensor(P + str(l) + "ksB", [128, 64], F32))
    tabs = s1.enter_context(nc.sbuf_tensor(P + str(l) + "tabs", [128, 4 * 64], F32))
    tabu = s1.enter_context(nc.sbuf_tensor(P + str(l) + "tabu", [128, 32], F32))
    qkt = [s1.enter_context(nc.sbuf_tensor(P + str(l) + f"qkt{i}", [128, 1024], F32)) for i in range(2)]
    qro = [s1.enter_context(nc.sbuf_tensor(P + str(l) + f"qro{i}", [128, 1024], F32)) for i in range(2)]
    qtmp = s1.enter_context(nc.sbuf_tensor(P + str(l) + "qtmp", [128, 512], F32))
    rnq = s1.enter_context(nc.sbuf_tensor(P + str(l) + "rnq", [128, 16], F32))
    pq = s1.enter_context(nc.psum_tensor(P + str(l) + "pq", [128, 1536], F32))
    prep = s1.enter_context(nc.psum_tensor(P + str(l) + "prep", [128, 512], F32))

    syld = ctx.psem(P + "syld")
    tld = Ring(ctx, "hwD", 1)
    shf = ctx.psem(P + "shf")
    wql = ctx.psem(P + "wql")
    qsl = ctx.psem(P + "qsl")
    repd = ctx.psem(P + "repd")
    repe = ctx.psem(P + "repe")
    mmd = ctx.psem(P + "mmd")
    qev = ctx.psem(P + "qev")
    vev = ctx.psem(P + "vev")
    vq1 = ctx.psem(P + "vq1")
    sq1 = ctx.psem(P + "sq1")
    vq2 = ctx.psem(P + "vq2")
    qst = Ring(ctx, "swA", 2)

    @block.sync
    def _(sync):
        qsl.inc(sync.dma_start(out=qsrow[:], in_=AP(ctx.dram[f"qs{l}"].tensor, 0, [[64, 1], [1, 64]])), 16)
        qsl.inc(sync.dma_start(out=ksrow[:], in_=AP(ctx.dram[f"ks{l}"].tensor, 0, [[64, 1], [1, 64]])), 16)
        for cc in range(4):
            syld.inc(sync.dma_start_transpose(
                syT[:, cc * NTOK:(cc + 1) * NTOK],
                yin_bf[:, cc * 128:(cc + 1) * 128]), 16)
        for cc in range(4, 8):
            if cc > 4:
                shf.wait(sync, cc - 4)
            tld.inc(sync.dma_start_transpose(
                tmpT[:], yin_bf[:, cc * 128:(cc + 1) * 128]), cc - 4)

    @block.vector
    def _(v):
        for i, cc in enumerate(range(4, 8)):
            tld.wait(v, i)
            v.memset(syT[:, cc * NTOK: cc * NTOK + 1], 0.0)
            v.drain()
            shf.inc(v.tensor_copy(syT[:, cc * NTOK + 1: (cc + 1) * NTOK], tmpT[:, 0:NTOK - 1]))

    @block.gpsimd
    def _(g):
        for cc in range(8):
            wql.inc(g.dma_start(out=wqf[:, cc * 1536:(cc + 1) * 1536],
                                in_=ctx.dram[f"wqkvF{l}"][cc * 128:(cc + 1) * 128, :]), 16)

    @block.tensor
    def _(t):
        qsl.wait(t)
        t.matmul(prep[:, 0:64], sb["ones1"][:], qsrow[:], start=True, stop=True)
        repd.inc(t.matmul(prep[:, 64:128], sb["ones1"][:], ksrow[:], start=True, stop=True))

    @block.vector
    def _(v):
        repd.wait(v)
        v.tensor_scalar(qsrep[:], prep[:, 0:64], 8.0, None, op0=ALU.mult)
        v.tensor_copy(ksrep[:], prep[:, 64:128])
        v.drain()
        v.tensor_scalar(qsB[:, 0:32], qsrep[:, 32:64], -1.0, None, op0=ALU.mult)
        v.tensor_copy(qsB[:, 32:64], qsrep[:, 0:32])
        v.tensor_scalar(ksB[:, 0:32], ksrep[:, 32:64], -1.0, None, op0=ALU.mult)
        repe.inc(v.tensor_copy(ksB[:, 32:64], ksrep[:, 0:32]))

    @block.tensor
    def _(t):
        syld.wait(t)
        shf.wait(t)
        wql.wait(t)
        for ti in range(NT):
            if ti >= 1:
                qev.wait(t, ti)
                vev.wait(t, ti)
            for cc in range(8):
                lhsT = syT[:, cc * NTOK + ti * 128: cc * NTOK + (ti + 1) * 128]
                for g3 in range(3):
                    instr = t.matmul(pq[:, g3 * 512:(g3 + 1) * 512], lhsT,
                                     wqf[:, cc * 1536 + g3 * 512: cc * 1536 + (g3 + 1) * 512],
                                     start=(cc == 0), stop=(cc == 7))
            mmd.inc(instr)

    @block.scalar
    def _(s):
        for ti in range(NT):
            mmd.wait(s, ti + 1)
            vev.inc(s.copy(V[:, ti * 512:(ti + 1) * 512], pq[:, 1024:1536]))
            vq1.wait(s, ti + 1)
            s.activation(rnq[:, 0:8], rnq[:, 0:8], AF.Ln, bias=1e-30)
            s.activation(rnq[:, 8:16], rnq[:, 8:16], AF.Ln, bias=1e-30)
            s.drain()
            s.activation(rnq[:, 0:8], rnq[:, 0:8], AF.Exp, scale=-0.5)
            sq1.inc(s.activation(rnq[:, 8:16], rnq[:, 8:16], AF.Exp, scale=-0.5))

    @block.vector
    def _(v):
        repe.wait(v)
        tabl.wait(v)
        for ti in range(NT):
            pr = ti % 2
            tc = slice(ti * 32, (ti + 1) * 32)
            mmd.wait(v, ti + 1)
            if ti >= 2:
                qst.wait(v, ti - 2, k=32)
            qev.inc(v.tensor_copy(qkt[pr][:], pq[:, 0:1024]))
            v.drain()
            v.tensor_mul(qtmp[:], qkt[pr][:, 0:512], qkt[pr][:, 0:512])
            v.drain()
            v.tensor_reduce(rnq[:, 0:8], AP(qtmp, 0, [[512, 128], [64, 8], [1, 64]]),
                            axis=AX, op=ALU.add)
            v.drain()
            v.tensor_mul(qtmp[:], qkt[pr][:, 512:1024], qkt[pr][:, 512:1024])
            v.drain()
            vq1.inc(v.tensor_reduce(rnq[:, 8:16], AP(qtmp, 0, [[512, 128], [64, 8], [1, 64]]),
                                    axis=AX, op=ALU.add))
            v.tensor_mul(tabu[:], sb["Tcos"][:, tc], sb["Txs"][:, tc])
            v.drain()
            v.tensor_mul(tabs[:, 0:32], tabu[:], qsrep[:, 0:32])
            v.tensor_mul(tabs[:, 32:64], tabu[:], qsrep[:, 32:64])
            v.drain()
            v.tensor_mul(tabu[:], sb["Tsin"][:, tc], sb["Txs"][:, tc])
            v.drain()
            v.tensor_mul(tabs[:, 64:96], tabu[:], qsB[:, 0:32])
            v.tensor_mul(tabs[:, 96:128], tabu[:], qsB[:, 32:64])
            v.drain()
            v.tensor_mul(tabu[:], sb["Tcos"][:, tc], sb["Trxs"][:, tc])
            v.drain()
            v.tensor_mul(tabs[:, 128:160], tabu[:], ksrep[:, 0:32])
            v.tensor_mul(tabs[:, 160:192], tabu[:], ksrep[:, 32:64])
            v.drain()
            v.tensor_mul(tabu[:], sb["Tsin"][:, tc], sb["Trxs"][:, tc])
            v.drain()
            v.tensor_mul(tabs[:, 192:224], tabu[:], ksB[:, 0:32])
            v.tensor_mul(tabs[:, 224:256], tabu[:], ksB[:, 32:64])
            sq1.wait(v, ti + 1)
            v.tensor_scalar_min(rnq[:, 0:8], rnq[:, 0:8], 1e12)
            v.tensor_scalar_min(rnq[:, 8:16], rnq[:, 8:16], 1e12)
            v.drain()
            v.tensor_tensor(qkt[pr][:, 0:512], qkt[pr][:, 0:512],
                            AP(rnq, 0, [[16, 128], [1, 8], [0, 64]]), op=ALU.mult)
            v.tensor_tensor(qkt[pr][:, 512:1024], qkt[pr][:, 512:1024],
                            AP(rnq, 8, [[16, 128], [1, 8], [0, 64]]), op=ALU.mult)
            v.drain()
            v.tensor_tensor(qro[pr][:, 0:512], qkt[pr][:, 0:512],
                            AP(tabs, 0, [[256, 128], [0, 8], [1, 64]]), op=ALU.mult)
            v.tensor_tensor(qtmp[:], AP(qkt[pr], 32, [[1024, 128], [64, 8], [-32, 2], [1, 32]]),
                            AP(tabs, 64, [[256, 128], [0, 8], [1, 64]]), op=ALU.mult)
            v.drain()
            v.tensor_add(qro[pr][:, 0:512], qro[pr][:, 0:512], qtmp[:])
            v.tensor_tensor(qro[pr][:, 512:1024], qkt[pr][:, 512:1024],
                            AP(tabs, 128, [[256, 128], [0, 8], [1, 64]]), op=ALU.mult)
            v.drain()
            v.tensor_tensor(qtmp[:], AP(qkt[pr], 544, [[1024, 128], [64, 8], [-32, 2], [1, 32]]),
                            AP(tabs, 192, [[256, 128], [0, 8], [1, 64]]), op=ALU.mult)
            v.drain()
            vq2.inc(v.tensor_add(qro[pr][:, 512:1024], qro[pr][:, 512:1024], qtmp[:]))

    @block.gpsimd
    def _(g):
        for ti in range(NT):
            vq2.wait(g, ti + 1)
            qst.inc(g.dma_start(out=qr_bf[ti * 128:(ti + 1) * 128, :],
                                in_=qro[ti % 2][:, 0:512]), ti)
            qst.inc(g.dma_start(out=kr_bf[ti * 128:(ti + 1) * 128, :],
                                in_=qro[ti % 2][:, 512:1024]), ti)


    s1.close()

    if _os.environ.get("K_NO_A3") == "1":
        return
    KT2 = ph.enter_context(nc.sbuf_tensor(P + str(l) + "KT2", [128, 4 * NTOK], BF16))
    QT2 = ph.enter_context(nc.sbuf_tensor(P + str(l) + "QT2", [128, 4 * NTOK], BF16))
    wo2 = ph.enter_context(nc.sbuf_tensor(P + str(l) + "wo2", [64, 8 * D], BF16))
    SSl = ph.enter_context(nc.sbuf_tensor(P + str(l) + "SSl", [128, 640], F32))
    Pl = [ph.enter_context(nc.sbuf_tensor(P + str(l) + f"Pl{i}", [128, 640], BF16)) for i in range(3)]
    PTl = [ph.enter_context(nc.sbuf_tensor(P + str(l) + f"PTl{i}", [128, 640], BF16)) for i in range(2)]
    OTl = ph.enter_context(nc.sbuf_tensor(P + str(l) + "OTl", [64, 8 * 128], BF16))
    sstl = [ph.enter_context(nc.sbuf_tensor(P + str(l) + f"sstl{i}", [128, 8], F32)) for i in range(3)]
    yint = [ph.enter_context(nc.sbuf_tensor(P + str(l) + f"yin{i}", [128, D], F32)) for i in range(2)]
    y1t = [ph.enter_context(nc.sbuf_tensor(P + str(l) + f"y1t{i}", [128, D], F32)) for i in range(2)]
    Spl = [ph.enter_context(nc.psum_tensor(P + str(l) + f"Spl{i}", [128, 512], F32)) for i in range(2)]
    ptpl = [ph.enter_context(nc.psum_tensor(P + str(l) + f"ptpl{i}", [128, 1024], BF16)) for i in range(2)]
    otl = [ph.enter_context(nc.psum_tensor(P + str(l) + f"otl{i}", [128, 512], F32)) for i in range(2)]
    oppl = ph.enter_context(nc.psum_tensor(P + str(l) + "oppl", [128, 1024], F32))

    k2l = ctx.psem(P + "k2l")
    w2l = ctx.psem(P + "w2l")
    yld = Ring(ctx, "hwB", 2)
    lsm = ctx.psem(P + "lsm")
    lmc = ctx.psem(P + "lmc")
    lms = ctx.psem(P + "lms")
    lse = ctx.psem(P + "lse")
    lvm = ctx.psem(P + "lvm")
    lex = ctx.psem(P + "lex")
    lrz = ctx.psem(P + "lrz")
    lpn = ctx.psem(P + "lpn")
    ltt = ctx.psem(P + "ltt")
    lev = ctx.psem(P + "lev")
    lpv = ctx.psem(P + "lpv")
    lot = ctx.psem(P + "lot")
    lop = ctx.psem(P + "lop")
    lvx = ctx.psem(P + "lvx")
    lst = Ring(ctx, "hwC", 2)

    def ktile0(qt):
        return max(0, qt - 4)

    NU_l = NT * 8
    lunits = [divmod(u, 8) for u in range(NU_l)]

    def lchw(qt):
        klen = (qt - ktile0(qt) + 1) * 128
        return [512, klen - 512] if klen > 512 else [klen]

    cum_lch = [0]
    for (qt, h) in lunits:
        cum_lch.append(cum_lch[-1] + len(lchw(qt)))

    @block.sync
    def _(sync):
        qst.wait_all_issued(sync)
        for p in range(4):
            k2l.inc(sync.dma_start_transpose(
                KT2[:, p * NTOK:(p + 1) * NTOK], kr_bf[:, p * 128:(p + 1) * 128]), 16)
            k2l.inc(sync.dma_start_transpose(
                QT2[:, p * NTOK:(p + 1) * NTOK], qr_bf[:, p * 128:(p + 1) * 128]), 16)
        for qt in range(NT):
            if qt >= 2:
                lvx.wait(sync, qt - 1)
            yld.inc(sync.dma_start(out=yint[qt % 2][:], in_=yin_f[qt * 128:(qt + 1) * 128, :]), qt)
            if qt >= 1:
                lvx.wait(sync, qt)
                lst.inc(sync.dma_start(out=ymid_f[(qt - 1) * 128:qt * 128, :],
                                       in_=y1t[(qt - 1) % 2][:]), qt - 1)
        lvx.wait(sync, NT)
        lst.inc(sync.dma_start(out=ymid_f[(NT - 1) * 128:NT * 128, :], in_=y1t[(NT - 1) % 2][:]), NT - 1)

    @block.gpsimd
    def _(g):
        qst.wait_all_issued(g)
        for hh in range(8):
            w2l.inc(g.dma_start(out=wo2[0:64, hh * D:(hh + 1) * D],
                                in_=ctx.dram[f"wo{l}"][hh * 64:(hh + 1) * 64, :]), 16)

    @block.tensor
    def _(t):
        k2l.wait(t)
        w2l.wait(t)

        def t_smm(u):
            qt, h = lunits[u]
            p, ho = divmod(h, 2)
            rb = 64 * ho
            kt0 = ktile0(qt)
            for c, w in enumerate(lchw(qt)):
                gidx = cum_lch[u] + c
                if gidx >= 2:
                    lse.wait(t, gidx - 1)
                off = p * NTOK + kt0 * 128 + c * 512
                instr = t.matmul(Spl[gidx % 2][:, :w],
                                 QT2[rb:rb + 64, p * NTOK + qt * 128: p * NTOK + (qt + 1) * 128],
                                 KT2[rb:rb + 64, off: off + w],
                                 start=True, stop=True)
                lsm.inc(instr)

        def t_back(u):
            qt, h = lunits[u]
            kt0 = ktile0(qt)
            nkt = qt - kt0 + 1
            lpn.wait(t, u + 1)
            if u >= 2:
                lev.wait(t, u - 1)
            for ci in range(nkt):
                instr = t.transpose(ptpl[u % 2][:, ci * 128:(ci + 1) * 128],
                                    Pl[u % 3][:, ci * 128:(ci + 1) * 128], sb["ident"][:])
            ltt.inc(instr)
            lev.wait(t, u + 1)
            if u >= 2:
                lot.wait(t, u - 1)
            for ci in range(nkt):
                kt = kt0 + ci
                instr = t.matmul(otl[u % 2][0:64, 0:128],
                                 V[:, kt * 512 + h * 64: kt * 512 + (h + 1) * 64],
                                 PTl[u % 2][:, ci * 128:(ci + 1) * 128],
                                 start=(ci == 0), stop=(ci == nkt - 1))
            lpv.inc(instr)
            if h == 7:
                lot.wait(t, u + 1)
                if qt >= 1:
                    lvx.wait(t, qt)
                for hh in range(8):
                    for g2 in range(2):
                        instr = t.matmul(oppl[:, g2 * 512:(g2 + 1) * 512],
                                         OTl[0:64, hh * 128:(hh + 1) * 128],
                                         wo2[0:64, hh * D + g2 * 512: hh * D + (g2 + 1) * 512],
                                         start=(hh == 0), stop=(hh == 7))
                lop.inc(instr)

        for u in range(NU_l + 2):
            if u < NU_l:
                t_smm(u)
            if u >= 2:
                t_back(u - 2)

    @block.scalar
    def _(s):
        def s_exp(u):
            qt, h = lunits[u]
            klen = (qt - ktile0(qt) + 1) * 128
            lvm.wait(s, u + 1)
            lms.wait(s, u + 1)
            if u >= 3:
                ltt.wait(s, u - 2)
                lpn.wait(s, u - 2)
                lrz.wait(s, u - 2)
            lex.inc(s.activation(Pl[u % 3][:, :klen], SSl[:, :klen], AF.Exp,
                                 bias=sstl[u % 3][:, 0:1], accum_out=sstl[u % 3][:, 1:2]))

        def s_evac(u):
            qt, h = lunits[u]
            if u >= 1:
                lms.wait(s, u)
                lmc.wait(s, u)
            for c, w in enumerate(lchw(qt)):
                gidx = cum_lch[u] + c
                lsm.wait(s, gidx + 1)
                lse.inc(s.copy(SSl[:, c * 512:c * 512 + w], Spl[gidx % 2][:, :w]))

        for u in range(NU_l + 2):
            if u >= 1 and u - 1 < NU_l:
                s_exp(u - 1)
            if u < NU_l:
                s_evac(u)

    @block.vector
    def _(v):
        def v_s1(u):
            qt, h = lunits[u]
            klen = (qt - ktile0(qt) + 1) * 128
            sp_ = sstl[u % 3]
            lse.wait(v, cum_lch[u + 1])
            lmc.inc(v.reduce_max(sp_[:, 3:4], SSl[:, :klen], axis=AX))
            v.drain()
            if u >= 3:
                lex.wait(v, u - 2)
            lvm.inc(v.tensor_scalar(sp_[:, 0:1], sp_[:, 3:4], -1.0, None, op0=ALU.mult))
            if u >= 1:
                lex.wait(v, u)
            instr = v.tensor_add(SSl[:, klen - 128:klen], SSl[:, klen - 128:klen], sb["mdiag"][:])
            if qt >= 4:
                instr = v.tensor_add(SSl[:, 0:128], SSl[:, 0:128], sb["mfar"][:])
            lms.inc(instr)

        def v_s2(u):
            qt, h = lunits[u]
            klen = (qt - ktile0(qt) + 1) * 128
            sp_ = sstl[u % 3]
            lex.wait(v, u + 1)
            if u >= 3:
                lpn.wait(v, u - 2)
            lrz.inc(v.reciprocal(sp_[:, 2:3], sp_[:, 1:2]))
            v.drain()
            lpn.inc(v.tensor_scalar(Pl[u % 3][:, :klen], Pl[u % 3][:, :klen],
                                    sp_[:, 2:3], None, op0=ALU.mult))

        def v_s3(u):
            qt, h = lunits[u]
            klen = (qt - ktile0(qt) + 1) * 128
            ltt.wait(v, u + 1)
            if u >= 2:
                lpv.wait(v, u - 1)
            lev.inc(v.tensor_copy(PTl[u % 2][:, :klen], ptpl[u % 2][:, :klen]))
            lpv.wait(v, u + 1)
            if qt >= 1:
                lop.wait(v, qt)
            lot.inc(v.tensor_copy(OTl[0:64, h * 128:(h + 1) * 128], otl[u % 2][0:64, 0:128]))
            if h == 7:
                lop.wait(v, qt + 1)
                yld.wait(v, qt)
                if qt >= 2:
                    lst.wait(v, qt - 2)
                lvx.inc(v.tensor_add(y1t[qt % 2][:], oppl[:], yint[qt % 2][:]))

        for u in range(NU_l + 2):
            if u < NU_l:
                v_s1(u)
            if 1 <= u and u - 1 < NU_l:
                v_s2(u - 1)
            if u >= 2:
                v_s3(u - 2)


def emit_p2b(ctx, l):
    nc, block, sb = ctx.nc, ctx.block, ctx.sb
    ph = ctx.phase_stack
    ymid_f = ctx.dram["yB_f"]
    z2_bf = ctx.dram["z2_bf"]
    P = "m"
    SUP = 4
    TSUP = NTOK // SUP
    TPS = TSUP // 128

    lngrep = ph.enter_context(nc.sbuf_tensor(P + str(l) + "lngrep", [128, D], F32))
    pa = ph.enter_context(nc.psum_tensor(P + str(l) + "pa", [128, 1536], F32))
    pg = ph.enter_context(nc.psum_tensor(P + str(l) + "pg", [128, 1536], F32))
    pd = ph.enter_context(nc.psum_tensor(P + str(l) + "pd", [128, 1024], F32))
    s0 = ExitStack()
    lngrow = s0.enter_context(nc.sbuf_tensor(P + str(l) + "lngrow", [1, D], F32))
    xt = [s0.enter_context(nc.sbuf_tensor(P + str(l) + f"x{i}", [128, D], F32)) for i in range(2)]
    z2t = [s0.enter_context(nc.sbuf_tensor(P + str(l) + f"z2{i}", [128, D], F32)) for i in range(2)]
    dump = s0.enter_context(nc.sbuf_tensor(P + str(l) + "dump", [128, D], F32))
    stt_ = [s0.enter_context(nc.sbuf_tensor(P + str(l) + f"st{i}", [128, 4], F32)) for i in range(2)]

    lngl = ctx.psem(P + "lngl")
    lrep = ctx.psem(P + "lrep")
    lrepe = ctx.psem(P + "lrepe")
    bld = ctx.psem(P + "bld")
    zld = Ring(ctx, "hwA", 2)
    zcf = ctx.psem(P + "zcf")
    zss = ctx.psem(P + "zss")
    zzd = ctx.psem(P + "zzd")
    zst = Ring(ctx, "swA", 2)

    @block.sync
    def _(sync):
        lngl.inc(sync.dma_start(out=lngrow[:], in_=AP(ctx.dram[f"lng{l}"].tensor, 0, [[D, 1], [1, D]])), 16)
        for ti in range(NT):
            if ti >= 2:
                zzd.wait(sync, ti - 1)
            zld.inc(sync.dma_start(out=xt[ti % 2][:, 0:512],
                                   in_=ymid_f[ti * 128:(ti + 1) * 128, 0:512]), ti)
            if ti == 0:
                zld.inc(sync.dma_start(out=xt[0][1:128, 512:1024],
                                       in_=ymid_f[0:127, 512:1024]), ti)
            else:
                zld.inc(sync.dma_start(out=xt[ti % 2][:, 512:1024],
                                       in_=ymid_f[ti * 128 - 1:(ti + 1) * 128 - 1, 512:1024]), ti)

    @block.tensor
    def _(t):
        lngl.wait(t)
        t.matmul(pa[:, 0:512], sb["ones1"][:], lngrow[:, 0:512], start=True, stop=True)
        lrep.inc(t.matmul(pa[:, 512:1024], sb["ones1"][:], lngrow[:, 512:1024], start=True, stop=True))

    @block.vector
    def _(v):
        lrep.wait(v)
        lrepe.inc(v.tensor_copy(lngrep[:], pa[:, 0:1024]))
        v.memset(xt[0][0:1, 512:1024], 0.0)
        v.drain()
        for ti in range(NT):
            x, s_ = xt[ti % 2], stt_[ti % 2]
            zld.wait(v, ti, k=32)
            v.reduce_sum(s_[:, 0:1], x[:], axis=AX)
            v.drain()
            v.tensor_scalar(s_[:, 0:1], s_[:, 0:1], 1.0 / D, None, op0=ALU.mult)
            v.drain()
            zcf.inc(v.tensor_scalar(x[:], x[:], s_[:, 0:1], None, op0=ALU.subtract))
            v.drain()
            zss.wait(v, ti + 1)
            if ti >= 2:
                zst.wait(v, ti - 2)
            v.tensor_scalar(x[:], x[:], s_[:, 2:3], None, op0=ALU.mult)
            v.drain()
            zzd.inc(v.tensor_mul(z2t[ti % 2][:], x[:], lngrep[:]))

    @block.scalar
    def _(s):
        for ti in range(NT):
            zcf.wait(s, ti + 1)
            if ti >= 2:
                zzd.wait(s, ti - 1)
            c, s_ = xt[ti % 2], stt_[ti % 2]
            s.activation(dump[:], c[:], AF.Square, accum_out=s_[:, 1:2])
            s.drain()
            s.activation(s_[:, 2:3], s_[:, 1:2], AF.Ln, scale=1.0 / D, bias=1e-5)
            s.drain()
            zss.inc(s.activation(s_[:, 2:3], s_[:, 2:3], AF.Exp, scale=-0.5))

    @block.gpsimd
    def _(g):
        for ti in range(NT):
            zzd.wait(g, ti + 1)
            zst.inc(g.dma_start(out=z2_bf[ti * 128:(ti + 1) * 128, :], in_=z2t[ti % 2][:]), ti)

    s0.close()

    z2T = ph.enter_context(nc.sbuf_tensor(P + str(l) + "z2T", [128, 8 * TSUP], BF16))
    w1sl = [ph.enter_context(nc.sbuf_tensor(P + str(l) + f"w1sl{i}", [128, 8 * 128], BF16)) for i in range(2)]
    gatedT = ph.enter_context(nc.sbuf_tensor(P + str(l) + "gatedT", [128, NJ * TSUP], BF16))
    w2sb = ph.enter_context(nc.sbuf_tensor(P + str(l) + "w2sb", [128, NJ * D], BF16))
    geluSb = ph.enter_context(nc.sbuf_tensor(P + str(l) + "geluSb", [128, TSUP], F32))
    b1eff = ph.enter_context(nc.sbuf_tensor(P + str(l) + "b1eff", [128, 2 * NJ], F32))
    b1ld = ph.enter_context(nc.sbuf_tensor(P + str(l) + "b1ld", [128, 2 * NJ], F32))
    lbS = ph.enter_context(nc.sbuf_tensor(P + str(l) + "lbS", [128, 8], BF16))
    b2row = ph.enter_context(nc.sbuf_tensor(P + str(l) + "b2row", [1, D], F32))
    ymt = [ph.enter_context(nc.sbuf_tensor(P + str(l) + f"ym{i}", [128, D], F32)) for i in range(2)]
    x1t2 = [ph.enter_context(nc.sbuf_tensor(P + str(l) + f"x1b{i}", [128, D], F32)) for i in range(2)]
    yot = [ph.enter_context(nc.sbuf_tensor(P + str(l) + f"yo{i}", [128, D], F32)) for i in range(2)]

    w2ld = ctx.psem(P + "w2ld")
    ztl = ctx.psem(P + "ztl")
    wsl = Ring(ctx, "swB", 2)
    tam = ctx.psem(P + "tam")
    tgm = ctx.psem(P + "tgm")
    bad = ctx.psem(P + "bad")
    bgd = ctx.psem(P + "bgd")
    gld = ctx.psem(P + "gld")
    gtd = ctx.psem(P + "gtd")
    wdm = ctx.psem(P + "wdm")
    wev = ctx.psem(P + "wev")
    yld2 = Ring(ctx, "hwB", 2)
    ost = Ring(ctx, "hwC", 2)
    ostg = Ring(ctx, "swC", 2)
    ostfin = ctx.sem("out_store")

    nslab = 2 * NJ

    @block.sync
    def _(sync):
        bld.inc(sync.dma_start(out=b1ld[:, 0:NJ],
                               in_=AP(ctx.dram[f"b1a{l}"].tensor, 0, [[1, 128], [128, NJ]]),
                               allow_slow_non_contiguous=True), 16)
        bld.inc(sync.dma_start(out=b1ld[:, NJ:2 * NJ],
                               in_=AP(ctx.dram[f"b1g{l}"].tensor, 0, [[1, 128], [128, NJ]]),
                               allow_slow_non_contiguous=True), 16)
        bld.inc(sync.dma_start(out=b2row[:],
                               in_=AP(ctx.dram[f"b2{l}"].tensor, 0, [[D, 1], [1, D]])), 16)
        for s in range(SUP):
            if s >= 1:
                tgm.wait(sync, s * NJ)
            zst.wait_all_issued(sync)
            for cc in range(8):
                ztl.inc(sync.dma_start_transpose(
                    z2T[:, cc * TSUP:(cc + 1) * TSUP],
                    z2_bf[s * TSUP:(s + 1) * TSUP, cc * 128:(cc + 1) * 128]), 16)
            for ti in range(TPS):
                gt = s * TPS + ti
                if gt >= 2:
                    wev.wait(sync, gt - 1)
                yld2.inc(sync.dma_start(out=ymt[gt % 2][:],
                                        in_=ymid_f[gt * 128:(gt + 1) * 128, :]), gt)
                if l == 1:
                    yld2.inc(sync.dma_start(out=x1t2[gt % 2][:],
                                            in_=ctx.dram["x1_f"][gt * 128:(gt + 1) * 128, :]), gt)
                wev.wait(sync, gt + 1)
                dst = ctx.dram["out"] if l == 1 else ctx.dram["yA_f"]
                inst = sync.dma_start(out=dst[gt * 128:(gt + 1) * 128, :], in_=yot[gt % 2][:])
                ost.inc(inst, gt)
                if l == 1:
                    ostfin.n += 0
        if l == 1:
            ost.wait_all_issued(sync)
            ostfin.inc(sync.nop())

    bldg = ctx.psem(P + "bldg")

    @block.gpsimd
    def _(g):
        bldg.inc(g.dma_start(out=lbS[:],
                             in_=AP(ctx.dram[f"lnb{l}"].tensor, 0, [[1, 128], [128, 8]]),
                             allow_slow_non_contiguous=True), 16)
        for jj in range(NJ):
            w2ld.inc(g.dma_start(out=w2sb[:, jj * D:(jj + 1) * D],
                                 in_=ctx.dram[f"w2{l}"][jj * 128:(jj + 1) * 128, :]), 16)
        def emit_bf_stores(sprev):
            if l != 0:
                return
            for ti in range(TPS):
                gt = sprev * TPS + ti
                wev.wait(g, gt + 1)
                ostg.inc(g.dma_start(out=ctx.dram["yA_bf"][gt * 128:(gt + 1) * 128, :],
                                     in_=yot[gt % 2][:]), gt)

        for s in range(SUP):
            if s >= 1:
                emit_bf_stores(s - 1)
            for j in range(NJ):
                for ab in range(2):
                    k = s * nslab + 2 * j + ab
                    if k >= 2:
                        (tam if ab == 0 else tgm).wait(g, s * NJ + j)
                    src = ctx.dram[f"w1a{l}"] if ab == 0 else ctx.dram[f"w1g{l}"]
                    wsl.inc(g.dma_start(
                        out=w1sl[k % 2][:],
                        in_=AP(src.tensor, j * 128, [[HIDP, 128], [128 * HIDP, 8], [1, 128]])), k)
        emit_bf_stores(SUP - 1)

    @block.tensor
    def _(t):
        w2ld.wait(t)
        bld.wait(t)
        bldg.wait(t)
        for s in range(SUP):
            ztl.wait(t, 16 * 8 * (s + 1))
            for j in range(NJ):
                sj = s * NJ + j
                for ab in range(2):
                    k = s * nslab + 2 * j + ab
                    wsl.wait(t, k)
                    pp = pa if ab == 0 else pg
                    if sj >= 1:
                        gtd.wait(t, sj)
                        gld.wait(t, sj)
                    for cc in range(8):
                        lhsT = w1sl[k % 2][:, cc * 128:(cc + 1) * 128]
                        for tg in range(2):
                            instr = t.matmul(pp[:, tg * 512:(tg + 1) * 512], lhsT,
                                             z2T[:, cc * TSUP + tg * 512: cc * TSUP + (tg + 1) * 512],
                                             start=(cc == 0), stop=(cc == 7))
                        if s == 0:
                            instr = t.matmul(pp[:, 1024:1025], lhsT, lbS[:, cc:cc + 1],
                                             start=(cc == 0), stop=(cc == 7))
                    if ab == 0:
                        tam.inc(instr)
                    else:
                        tgm.inc(instr)
            for ti in range(TPS):
                gt = s * TPS + ti
                if gt >= 1:
                    wev.wait(t, gt)
                for g2 in range(2):
                    t.matmul(pd[:, g2 * 512:(g2 + 1) * 512], sb["ones1"][:],
                             b2row[:, g2 * 512:(g2 + 1) * 512], start=True, stop=False)
                gtd.wait(t, (s + 1) * NJ)
                for j in range(NJ):
                    lhsT = gatedT[:, j * TSUP + ti * 128: j * TSUP + (ti + 1) * 128]
                    for g2 in range(2):
                        instr = t.matmul(pd[:, g2 * 512:(g2 + 1) * 512], lhsT,
                                         w2sb[:, j * D + g2 * 512: j * D + (g2 + 1) * 512],
                                         start=False, stop=(j == NJ - 1))
                wdm.inc(instr)

    @block.scalar
    def _(s_e):
        for s in range(SUP):
            for j in range(NJ):
                sj = s * NJ + j
                tgm.wait(s_e, sj + 1)
                if sj >= 1:
                    gtd.wait(s_e, sj)
                if s == 0:
                    bgd.wait(s_e, j + 1)
                gld.inc(s_e.activation(geluSb[:], pg[:, 0:TSUP], AF.Gelu,
                                       bias=b1eff[:, NJ + j:NJ + j + 1]))

    @block.vector
    def _(v):
        lrepe.wait(v, 1)
        bld.wait(v, 48)
        for s in range(SUP):
            for j in range(NJ):
                sj = s * NJ + j
                if s == 0:
                    tam.wait(v, sj + 1)
                    bad.inc(v.tensor_add(b1eff[:, j:j + 1], pa[:, 1024:1025], b1ld[:, j:j + 1]))
                    tgm.wait(v, sj + 1)
                    bgd.inc(v.tensor_add(b1eff[:, NJ + j:NJ + j + 1], pg[:, 1024:1025],
                                         b1ld[:, NJ + j:NJ + j + 1]))
                    v.drain()
                if s >= 1:
                    wdm.wait(v, s * TPS)
                gld.wait(v, sj + 1)
                tam.wait(v, sj + 1)
                gtd.inc(v.scalar_tensor_tensor(
                    gatedT[:, j * TSUP:(j + 1) * TSUP], pa[:, 0:TSUP],
                    b1eff[:, j:j + 1], geluSb[:], op0=ALU.add, op1=ALU.mult))
            for ti in range(TPS):
                gt = s * TPS + ti
                wdm.wait(v, gt + 1)
                yld2.wait(v, gt, k=(32 if l == 1 else 16))
                if gt >= 2:
                    ost.wait(v, gt - 2)
                    if l == 0:
                        ostg.wait(v, gt - 2)
                inst = v.tensor_add(yot[gt % 2][:], pd[:], ymt[gt % 2][:])
                if l == 1:
                    v.drain()
                    inst = v.tensor_add(yot[gt % 2][:], yot[gt % 2][:], x1t2[gt % 2][:])
                wev.inc(inst)



def _prep_inputs(inputs):
    f32 = np.float32
    x = np.ascontiguousarray(inputs["L0_attn_x"])
    w1 = inputs["L0_ffn_w1"]
    base = {
        "wqkvA": np.ascontiguousarray(inputs["L0_attn_w_qkv"].T),
        "gammaA": np.ascontiguousarray(inputs["L0_attn_gamma"]),
        "woutT": np.ascontiguousarray(inputs["L0_attn_w_out"].T),
    }
    for l in range(2):
        base[f"wqkvF{l}"] = np.ascontiguousarray(inputs["L0_ffn_wqkv"][l])
        base[f"qs{l}"] = np.ascontiguousarray(inputs["L0_ffn_q_scale"][l])
        base[f"ks{l}"] = np.ascontiguousarray(inputs["L0_ffn_k_scale"][l])
        base[f"wo{l}"] = np.ascontiguousarray(inputs["L0_ffn_wo"][l])
        base[f"lng{l}"] = np.ascontiguousarray(inputs["L0_ffn_ln_g"][l])
        base[f"lnb{l}"] = np.ascontiguousarray(inputs["L0_ffn_ln_b"][l])
        w1a = np.zeros((D, HIDP), f32)
        w1g = np.zeros((D, HIDP), f32)
        w1a[:, :HID] = w1[l][:, :HID]
        w1g[:, :HID] = w1[l][:, HID:]
        base[f"w1a{l}"] = w1a
        base[f"w1g{l}"] = w1g
        b1a = np.zeros((HIDP,), f32)
        b1g = np.zeros((HIDP,), f32)
        b1a[:HID] = inputs["L0_ffn_b1"][l][:HID]
        b1g[:HID] = inputs["L0_ffn_b1"][l][HID:]
        base[f"b1a{l}"] = b1a
        base[f"b1g{l}"] = b1g
        w2p = np.zeros((HIDP, D), f32)
        w2p[:HID] = inputs["L0_ffn_w2"][l]
        base[f"w2{l}"] = w2p
        base[f"b2{l}"] = np.ascontiguousarray(inputs["L0_ffn_b2"][l])
    in_maps = []
    for c in range(NCORE):
        m = dict(base)
        m["x"] = np.ascontiguousarray(x[c])
        in_maps.append(m)
    return in_maps


def kernel(**inputs):
    key = (tuple(_TAPS), _PHASES)
    if key not in _BUILD_CACHE:
        _BUILD_CACHE[key] = build_nc(taps=_TAPS, phases=_PHASES)
    nc = _BUILD_CACHE[key]
    in_maps = _prep_inputs(inputs)
    res = run_bass_kernel_spmd(nc, in_maps, core_ids=list(range(NCORE)))
    out = np.stack([np.asarray(r["out"]) for r in res.results], axis=0)
    if _TAPS:
        kernel.last_taps = [{k: np.asarray(v) for k, v in r.items()} for r in res.results]
    return out
